# version 24
# baseline (speedup 1.0000x reference)
import numpy as np
import ml_dtypes
from contextlib import ExitStack
import concourse.bass as bass
import concourse.mybir as mybir
from concourse.bass_utils import run_bass_kernel_spmd

F32 = mybir.dt.float32
BF16 = mybir.dt.bfloat16
I32 = mybir.dt.int32
AF = mybir.ActivationFunctionType
ALU = mybir.AluOpType
AXX = mybir.AxisListType.X

S = 2048
T = 1024
DM = 4096
DI = 512
DFF = 2048
NL = 4
NTI = DI // 128
NTF = DFF // 128
NTV = DM // 128
NCK = S // 512
D = DM
H = 11008
DT = D // 128
HT = H // 128
NB = T // 512
DSW = 256
NDS = D // DSW
ABATCH = 4
EPS = 1e-5
ENGS = ("sync", "scalar", "vector", "tensor", "gpsimd")


class Sem:
    def __init__(self, nc, es, name):
        self.h = es.enter_context(nc.semaphore(name))
        self.v = 0

    def add(self, n=1):
        self.v += n
        return self.v


class Plan:
    def __init__(self):
        self.ops = {e: [] for e in ENGS}

    def op(self, eng, fn):
        self.ops[eng].append(fn)

    def wait(self, eng, sem, val):
        if val and val > 0:
            self.ops[eng].append(lambda e, s=sem.h, v=val: e.wait_ge(s, v))

    def replay(self, block):
        for eng in ENGS:
            def mkrun(eng):
                def run(e):
                    for fn in self.ops[eng]:
                        fn(e)
                return run
            getattr(block, eng)(mkrun(eng))


def emit_ffn(nc, P, es, x1, qw_g, z_g, s_g, qw_u, z_u, s_u, qw_d, z_d, s_d,
             out, act_sp, r2_sp, sems, pers):
    mk = lambda name, shape, dt: es.enter_context(nc.sbuf_tensor(name, shape, dt))
    s_ld, s_sc, s_wl, s_mm, s_dq, s_bc, s_ac, s_st, s_gp = (
        sems[k] for k in ("ld", "sc", "wl", "mm", "dq", "bc", "ac", "st", "gp"))
    s_wb = [sems["wb0"], sems["wb1"], sems["wb2"]]
    s_ci, s_co, s_cc = sems["ci"], sems["co"], sems["cc"]
    s_sp = [sems["sp0"], sems["sp1"], sems["sp2"]]
    s_ab = [sems["ab0"], sems["ab1"]]
    s_wd, s_r2 = sems["wd"], sems["r2"]
    s_fo = [sems["fo0"], sems["fo1"]]
    s_fl = [sems["fl0"], sems["fl1"]]
    s_ar = sems["ar"]
    ident_b = pers["ident_b"]
    x1f = mk("x1f", [128, D], F32)
    x1b = mk("x1b", [128, D], BF16)
    Asb = mk("Asb", [HT, T], BF16)
    pA, pb0, pb1, pg, pu = pers["pA"], pers["pb0"], pers["pb1"], pers["pg"], pers["pu"]
    v_ident = pers["v_ident"]

    gu = ExitStack()
    mkg = lambda name, shape, dt: gu.enter_context(nc.sbuf_tensor(name, shape, dt))
    with gu:
        sel32 = mkg("sel32", [32, 32 * 128], BF16)
        ind86 = mkg("ind86", [128, 86 * 86], BF16)
        x1T = mkg("x1T", [128, DT, T], BF16)
        wbuf = [mkg(f"wbuf{i}", [128, DT, 128], BF16) for i in range(3)]
        zb = mkg("zb", [32, 1024], I32)
        sbf_ = mkg("sbf_", [32, 1024], F32)
        nzs8 = {(m, i): mkg(f"nzs8{m}{i}", [32, 1024], BF16) for m in "gu" for i in range(2)}
        s8 = {(m, i): mkg(f"s8{m}{i}", [32, 1024], BF16) for m in "gu" for i in range(2)}
        Xs = mkg("Xs", [32, T], BF16)
        tb = mkg("tb", [128, T], BF16)
        actc = [mkg(f"actc{i}", [128, T], BF16) for i in range(3)]

        P.op("gpsimd", lambda e: e.iota(sel32[:], pattern=[[-1, 32], [0, 128]], base=0, channel_multiplier=1, allow_small_or_imprecise_dtypes=True))
        P.op("gpsimd", lambda e: e.iota(ind86[:], pattern=[[-1, 86], [1, 86]], base=0, channel_multiplier=0, allow_small_or_imprecise_dtypes=True).then_inc(s_gp.h, 1))
        v_io = s_gp.add(1)
        P.wait("vector", s_gp, v_io)
        P.op("vector", lambda e: e.tensor_scalar(out=sel32[:], in0=sel32[:], scalar1=0.0, scalar2=None, op0=ALU.is_equal))
        P.op("vector", lambda e: e.tensor_scalar(out=ind86[:], in0=ind86[:], scalar1=0.0, scalar2=None, op0=ALU.is_equal).then_inc(s_dq.h, 1))
        v_sel = s_dq.add(1)

        v_tileTr = {}
        v_round = 0
        for tt in range(T // 128):
            P.op("sync", lambda e, tt=tt: e.dma_start(out=x1f[:], in_=x1[tt * 128:(tt + 1) * 128, :]).then_inc(s_ld.h, 16))
            lv = s_ld.add(16)
            P.wait("vector", s_ld, lv)
            if tt > 0:
                P.wait("vector", s_mm, v_tileTr[tt - 1])
            P.op("vector", lambda e: e.tensor_copy(x1b[:], x1f[:]).then_inc(s_dq.h, 1))
            dv = s_dq.add(1)
            P.wait("sync", s_dq, dv)
            P.wait("tensor", s_dq, dv)
            if tt == 0:
                P.wait("tensor", s_dq, v_ident)
            for rnd in range(8):
                P.wait("tensor", s_dq, v_round)
                def trk(e, c, k, last):
                    ins = e.transpose(
                        pb0[:].bitcast(BF16)[:, k * 128:(k + 1) * 128],
                        x1b[:, c * 128:(c + 1) * 128], ident_b[:])
                    if last:
                        ins.then_inc(s_mm.h, 1)
                for k in range(4):
                    c = rnd * 4 + k
                    P.op("tensor", lambda e, c=c, k=k, last=(k == 3): trk(e, c, k, last))
                mv = s_mm.add(1)
                P.wait("vector", s_mm, mv)
                def evk(e, c, k, tt, last):
                    ins = e.tensor_copy(
                        x1T[:, c, tt * 128:(tt + 1) * 128],
                        pb0[:].bitcast(BF16)[:, k * 128:(k + 1) * 128])
                    if last:
                        ins.then_inc(s_dq.h, 1)
                for k in range(4):
                    c = rnd * 4 + k
                    P.op("vector", lambda e, c=c, k=k, tt=tt, last=(k == 3): evk(e, c, k, tt, last))
                v_round = s_dq.add(1)
            v_tileTr[tt] = s_mm.v
        v_x1T = v_round

        P.wait("tensor", s_dq, v_x1T)
        P.wait("tensor", s_dq, v_sel)
        def xsmm(e, c, nb, last):
            ins = e.matmul(
                pA[0:32, nb * 512:(nb + 1) * 512],
                ind86[:, c * 86:c * 86 + 32], x1T[:, c, nb * 512:(nb + 1) * 512],
                start=(c == 0), stop=(c == DT - 1))
            if last:
                ins.then_inc(s_mm.h, 1)
        for c in range(DT):
            for nb in range(NB):
                P.op("tensor", lambda e, c=c, nb=nb, last=(c == DT - 1 and nb == NB - 1): xsmm(e, c, nb, last))
        mv = s_mm.add(1)
        P.wait("vector", s_mm, mv)
        P.op("vector", lambda e: e.tensor_copy(Xs[:], pA[0:32, :]).then_inc(s_dq.h, 1))
        v_Xs = s_dq.add(1)

        nsteps = 2 * HT
        v_dma, v_accum, v_silu, v_act, v_spill, v_zs = {}, {}, {}, {}, {}, {}
        v_dqblk, v_bcast = {}, {}
        v_accum_h, v_silu_h, v_act_h = {}, {}, {}
        NBATCH = (HT + 7) // 8

        def mat_of(s):
            return "g" if s % 2 == 0 else "u"

        def emit_zs_batch(b):
            j0 = b * 8 * 128
            w = min(1024, H - j0)
            for m, zt, st_ in (("g", z_g, s_g), ("u", z_u, s_u)):
                P.op("scalar", lambda e, zt=zt, j0=j0, w=w: e.dma_start(out=zb[:, 0:w], in_=zt[:, j0:j0 + w]).then_inc(s_sc.h, 16))
                P.op("scalar", lambda e, st_=st_, j0=j0, w=w: e.dma_start(out=sbf_[:, 0:w], in_=st_[:, j0:j0 + w]).then_inc(s_sc.h, 16))
                scv = s_sc.add(32)
                P.wait("vector", s_sc, scv)
                if b >= 2:
                    P.wait("vector", s_mm, v_accum[2 * (8 * (b - 2) + 7) + 1])
                P.op("vector", lambda e, m=m, b=b, w=w: e.scalar_tensor_tensor(
                    out=nzs8[(m, b % 2)][:, 0:w], in0=zb[:, 0:w], scalar=-1.0, in1=sbf_[:, 0:w],
                    op0=ALU.mult, op1=ALU.mult))
                P.op("vector", lambda e, m=m, b=b, w=w: e.tensor_copy(
                    s8[(m, b % 2)][:, 0:w], sbf_[:, 0:w]).then_inc(s_dq.h, 1))
                v_zs[(m, b)] = s_dq.add(1)
                P.wait("scalar", s_dq, v_zs[(m, b)])

        def emit_dma(s):
            j = s // 2
            qw = qw_g if s % 2 == 0 else qw_u
            buf = wbuf[s % 3]
            if s >= 3:
                P.wait("gpsimd", s_mm, v_accum[s - 3])
            sem = s_wb[s % 3]
            src = qw[:, j * 128:(j + 1) * 128].rearrange("(c k) j -> k c j", k=128)
            for h in range(4):
                P.op("gpsimd", lambda e, src=src, buf=buf, h=h, sem=sem: e.dma_start(
                    out=buf[:, h * 8:(h + 1) * 8, :], in_=src[:, h * 8:(h + 1) * 8, :]).then_inc(sem.h, 16))
            v_dma[s] = (sem, sem.add(64))

        def emit_bcast_dequant(s):
            j = s // 2
            b = j // 8
            sb_t = s8[(mat_of(s), b % 2)]
            buf = wbuf[s % 3]
            P.wait("tensor", s_dq, v_zs[(mat_of(s), b)])
            P.wait("vector", v_dma[s][0], v_dma[s][1])
            for blk in range(8):
                bank = pb0 if blk % 2 == 0 else pb1
                prev = v_dqblk.get((s, blk - 2)) if blk >= 2 else v_dqblk.get((s - 1, 6 + blk))
                assert prev is not None or (s == 0 and blk < 2)
                P.wait("tensor", s_dq, prev)
                def bcmm(e, G, g, bank, sb_t, j, last):
                    ins = e.matmul(
                        bank[:, g * 128:(g + 1) * 128], sel32[:, G * 128:(G + 1) * 128],
                        sb_t[:, (j % 8) * 128:(j % 8 + 1) * 128], start=True, stop=True)
                    if last:
                        ins.then_inc(s_bc.h, 1)
                for g in range(4):
                    G = blk * 4 + g
                    P.op("tensor", lambda e, G=G, g=g, bank=bank, sb_t=sb_t, j=j, last=(g == 3): bcmm(e, G, g, bank, sb_t, j, last))
                bcv = s_bc.add(1)
                P.wait("vector", s_bc, bcv)
                P.op("vector", lambda e, blk=blk, buf=buf, bank=bank: e.tensor_tensor(
                    out=buf[:, blk * 4:(blk + 1) * 4, :].rearrange("k c j -> k (c j)"),
                    in0=buf[:, blk * 4:(blk + 1) * 4, :].rearrange("k c j -> k (c j)"),
                    in1=bank[:], op=ALU.mult).then_inc(s_dq.h, 1))
                v_dqblk[(s, blk)] = s_dq.add(1)

        def emit_accum(s):
            j = s // 2
            buf = wbuf[s % 3]
            ps = pg if s % 2 == 0 else pu
            nz = nzs8[(mat_of(s), (j // 8) % 2)]
            P.wait("tensor", s_dq, v_dqblk[(s, 7)])
            if s == 0:
                P.wait("tensor", s_dq, v_Xs)

            def cormm(e, nb, nz, j, ps):
                e.matmul(ps[:, nb * 512:(nb + 1) * 512], nz[:, (j % 8) * 128:(j % 8 + 1) * 128],
                         Xs[:, nb * 512:(nb + 1) * 512], start=False, stop=True).then_inc(s_mm.h, 1)
            for nb in range(NB):
                if s % 2 == 0 and j > 0:
                    P.wait("tensor", s_ac, v_silu_h[(j - 1, nb)])
                if s % 2 == 1 and j > 0:
                    P.wait("tensor", s_dq, v_act_h[(j - 1, nb)])
                for c in range(DT):
                    P.op("tensor", lambda e, c=c, nb=nb, buf=buf, ps=ps: e.matmul(
                        ps[:, nb * 512:(nb + 1) * 512], buf[:, c, :],
                        x1T[:, c, nb * 512:(nb + 1) * 512], start=(c == 0), stop=False))
                P.op("tensor", lambda e, nb=nb, nz=nz, j=j, ps=ps: cormm(e, nb, nz, j, ps))
                v_accum_h[(s, nb)] = s_mm.add(1)
            v_accum[s] = v_accum_h[(s, NB - 1)]

        def emit_As(j, inc=False):
            ac = actc[j % 3]
            P.wait("tensor", s_dq, v_act[j])

            def asmm(e, j, nb, ac, last):
                ins = e.matmul(
                    pA[0:86, nb * 512:(nb + 1) * 512], ind86[:, j * 86:j * 86 + 86],
                    ac[:, nb * 512:(nb + 1) * 512], start=(j == 0), stop=(j == HT - 1))
                if last:
                    ins.then_inc(s_mm.h, 1)
            for nb in range(NB):
                P.op("tensor", lambda e, j=j, nb=nb, ac=ac, last=(inc and nb == NB - 1): asmm(e, j, nb, ac, last))

        emit_zs_batch(0)
        emit_zs_batch(1)
        for s in range(3):
            emit_dma(s)
        emit_bcast_dequant(0)
        emit_bcast_dequant(1)

        for s in range(nsteps):
            j = s // 2
            if s % 2 == 0 and j % 8 == 0 and j > 0 and (j // 8 + 1) < NBATCH:
                emit_zs_batch(j // 8 + 1)
            if s % 2 == 1 and j >= 1:
                emit_As(j - 1)
            emit_accum(s)
            if s % 2 == 0:
                for nb in range(NB):
                    P.wait("scalar", s_mm, v_accum_h[(s, nb)])
                    if j >= 1:
                        P.wait("scalar", s_dq, v_act_h[(j - 1, nb)])
                    P.op("scalar", lambda e, nb=nb: e.activation(
                        tb[:, nb * 512:(nb + 1) * 512], pg[:, nb * 512:(nb + 1) * 512],
                        AF.Silu).then_inc(s_ac.h, 1))
                    v_silu_h[(j, nb)] = s_ac.add(1)
                v_silu[j] = v_silu_h[(j, NB - 1)]
            else:
                ac = actc[j % 3]
                for nb in range(NB):
                    P.wait("vector", s_mm, v_accum_h[(s, nb)])
                    P.wait("vector", s_ac, v_silu_h[(j, nb)])
                    if j >= 3 and nb == 0:
                        P.wait("vector", s_sp[j % 3], v_spill[j - 3][1])
                    P.op("vector", lambda e, ac=ac, nb=nb: e.tensor_tensor(
                        out=ac[:, nb * 512:(nb + 1) * 512], in0=tb[:, nb * 512:(nb + 1) * 512],
                        in1=pu[:, nb * 512:(nb + 1) * 512], op=ALU.mult).then_inc(s_dq.h, 1))
                    v_act_h[(j, nb)] = s_dq.add(1)
                v_act[j] = v_act_h[(j, NB - 1)]
                P.wait("sync", s_dq, v_act[j])
                sem = s_sp[j % 3]
                P.op("sync", lambda e, j=j, ac=ac, sem=sem: e.dma_start(out=act_sp[j * 128:(j + 1) * 128, :], in_=ac[:]).then_inc(sem.h, 16))
                v_spill[j] = (sem, sem.add(16))
            if s + 3 < nsteps:
                emit_dma(s + 3)
            if s + 2 < nsteps:
                emit_bcast_dequant(s + 2)

        emit_As(HT - 1, inc=True)
        mv = s_mm.add(1)
        P.wait("vector", s_mm, mv)
        P.op("vector", lambda e: e.tensor_copy(Asb[:], pA[0:86, :]).then_inc(s_dq.h, 1))
        v_As = s_dq.add(1)
        v_guDone = [v_accum[nsteps - 1], v_spill[HT - 1], v_spill[HT - 2], v_spill[HT - 3]]

    dn = ExitStack()
    mkd = lambda name, shape, dt: dn.enter_context(nc.sbuf_tensor(name, shape, dt))
    with dn:
        for eng in ENGS:
            P.wait(eng, s_mm, v_guDone[0])
            for sm, vv_ in v_guDone[1:]:
                P.wait(eng, sm, vv_)
            P.wait(eng, s_dq, v_As)
        sel86 = mkd("sel86", [86, 86 * 128], BF16)
        wd = mkd("wd", [128, HT, DSW], BF16)
        actd = [mkd(f"actd{i}", [128, ABATCH, T], BF16) for i in range(2)]
        RACT = 32
        AB0 = RACT // ABATCH
        actr = mkd("actr", [128, RACT, T], BF16)
        zdb = mkd("zdb", [86, DSW], I32)
        sdb = mkd("sdb", [86, DSW], F32)
        nzsd = mkd("nzsd", [86, DSW], BF16)
        sdbb = mkd("sdbb", [86, DSW], BF16)
        dstg = mkd("dstg", [128, T], BF16)
        dstg2 = mkd("dstg2", [128, T], BF16)
        r2st = mkd("r2st", [128, 8, DSW], F32)
        fstg = [actr[:, i * 8:(i + 1) * 8, :].bitcast(F32).rearrange("k c d -> k (c d)") for i in range(4)]
        colA = mkd("fcolA", [128, 8], F32)
        colB = mkd("fcolB", [128, 8], F32)
        colC = mkd("fcolC", [128, 8], F32)
        for g8 in range(8):
            P.op("scalar", lambda e, g8=g8: e.dma_start(
                out=actr[:, g8 * 4:(g8 + 1) * 4, :],
                in_=act_sp[g8 * 4 * 128:(g8 + 1) * 4 * 128, :].rearrange("(c k) t -> k c t", k=128)).then_inc(s_ar.h, 16))
        v_actr = s_ar.add(16 * 8)
        P.op("gpsimd", lambda e: e.iota(sel86[:], pattern=[[-1, 86], [0, 128]], base=0, channel_multiplier=1, allow_small_or_imprecise_dtypes=True).then_inc(s_gp.h, 1))
        v_io = s_gp.add(1)
        P.wait("vector", s_gp, v_io)
        P.op("vector", lambda e: e.tensor_scalar(out=sel86[:], in0=sel86[:], scalar1=0.0, scalar2=None, op0=ALU.is_equal).then_inc(s_dq.h, 1))
        v_sel86 = s_dq.add(1)

        NAB = (HT + ABATCH - 1) // ABATCH
        v_wdma, v_adma, v_abdone, v_dsacc, v_zsd, v_x1sl = {}, {}, {}, {}, {}, {}
        v_dqdj, v_bcd, v_evst, v_r2, v_trv = {}, {}, {}, {}, {}

        def wd_batches(ds):
            src = qw_d[:, ds * DSW:(ds + 1) * DSW].rearrange("(c k) d -> k c d", k=128)
            off = 0
            for bi in range(8):
                n = 11 if bi < 7 else 9
                if ds >= 1:
                    ab_last = (off + n - 1) // ABATCH
                    P.wait("gpsimd", s_mm, v_abdone[(ds - 1, ab_last)])
                P.op("gpsimd", lambda e, src=src, off=off, n=n: e.dma_start(
                    out=wd[:, off:off + n, :], in_=src[:, off:off + n, :]).then_inc(s_wd.h, 16))
                v_wdma[(ds, bi)] = s_wd.add(16)
                off += n

        def wd_ready(ds, j):
            return v_wdma[(ds, min(j // 11, 7))]

        def emit_down_zs(ds):
            P.op("scalar", lambda e, ds=ds: e.dma_start(out=zdb[:], in_=z_d[:, ds * DSW:(ds + 1) * DSW]).then_inc(s_sc.h, 16))
            P.op("scalar", lambda e, ds=ds: e.dma_start(out=sdb[:], in_=s_d[:, ds * DSW:(ds + 1) * DSW]).then_inc(s_sc.h, 16))
            scv = s_sc.add(32)
            P.wait("vector", s_sc, scv)
            if ds >= 1:
                P.wait("vector", s_bc, v_bcd[(ds - 1, HT - 1)])
                P.wait("vector", s_mm, v_dsacc[ds - 1])
            P.op("vector", lambda e: e.scalar_tensor_tensor(out=nzsd[:], in0=zdb[:], scalar=-1.0, in1=sdb[:], op0=ALU.mult, op1=ALU.mult))
            P.op("vector", lambda e: e.tensor_copy(sdbb[:], sdb[:]).then_inc(s_dq.h, 1))
            v_zsd[ds] = s_dq.add(1)
            P.wait("scalar", s_dq, v_zsd[ds])

        def emit_down_x1(ds):
            if ds >= 1:
                P.wait("sync", s_dq, v_evst[(ds - 1, 1)])
            P.op("sync", lambda e, ds=ds: e.dma_start(
                out=x1f[:].rearrange("k (c d) -> k c d", d=DSW)[:, 0:8, :],
                in_=x1[:, ds * DSW:(ds + 1) * DSW].rearrange("(c k) d -> k c d", k=128)).then_inc(s_ld.h, 16))
            v_x1sl[ds] = s_ld.add(16)

        def emit_down_act(ds, ab):
            j0 = ab * ABATCH
            n = min(ABATCH, HT - j0)
            key = (ds, ab - 2) if ab >= AB0 + 2 else ((ds - 1, NAB - 2 + (ab - AB0)) if ds >= 1 else None)
            if key:
                P.wait("scalar", s_mm, v_abdone[key])
            sem = s_ab[ab % 2]
            P.op("scalar", lambda e, j0=j0, n=n, ab=ab, sem=sem: e.dma_start(
                out=actd[ab % 2][:, 0:n, :],
                in_=act_sp[j0 * 128:(j0 + n) * 128, :].rearrange("(c k) t -> k c t", k=128)).then_inc(sem.h, 16))
            v_adma[(ds, ab)] = (sem, sem.add(16))

        v_psfree = {}
        for ds in range(NDS):
            wd_batches(ds)
            emit_down_zs(ds)
            emit_down_x1(ds)
            emit_down_act(ds, AB0)
            emit_down_act(ds, AB0 + 1)
            if ds == 0:
                P.wait("tensor", s_dq, v_sel86)
                P.wait("tensor", s_ar, v_actr)
            def emit_bcdq(j, ds=ds):
                bank = pb0 if j % 2 == 0 else pb1
                P.wait("tensor", s_dq, v_dqdj.get((ds, j - 2)) or (v_dqdj.get((ds - 1, HT - 2 + j)) if ds >= 1 else None))
                if j == 0:
                    P.wait("tensor", s_dq, v_zsd[ds])
                P.op("tensor", lambda e, j=j, bank=bank: e.matmul(
                    bank[:, 0:DSW], sel86[:, j * 128:(j + 1) * 128], sdbb[:],
                    start=True, stop=True).then_inc(s_bc.h, 1))
                v_bcd[(ds, j)] = s_bc.add(1)
                P.wait("vector", s_bc, v_bcd[(ds, j)])
                P.wait("vector", s_wd, wd_ready(ds, j))
                P.op("vector", lambda e, j=j, bank=bank: e.tensor_tensor(
                    out=wd[:, j, :], in0=wd[:, j, :], in1=bank[:, 0:DSW], op=ALU.mult).then_inc(s_dq.h, 1))
                v_dqdj[(ds, j)] = s_dq.add(1)

            emit_bcdq(0)
            for j in range(HT):
                ab = j // ABATCH
                if j + 1 < HT:
                    emit_bcdq(j + 1)
                P.wait("tensor", s_dq, v_dqdj[(ds, j)])
                if j >= RACT:
                    P.wait("tensor", v_adma[(ds, ab)][0], v_adma[(ds, ab)][1])
                if ds > 0 and j == 0:
                    P.wait("tensor", s_ac, v_psfree[ds - 1][0])
                    P.wait("tensor", s_dq, v_psfree[ds - 1][1])
                batch_end = (j % ABATCH == ABATCH - 1 or j == HT - 1)

                def dnmm(e, j, m, nb, ps, ab, last):
                    asrc = (actr[:, j, nb * 512:(nb + 1) * 512] if j < RACT
                            else actd[ab % 2][:, j % ABATCH, nb * 512:(nb + 1) * 512])
                    ins = e.matmul(
                        ps[:, nb * 512:(nb + 1) * 512], wd[:, j, m * 128:(m + 1) * 128],
                        asrc, start=(j == 0), stop=False)
                    if last:
                        ins.then_inc(s_mm.h, 1)
                for m in range(2):
                    ps = pg if m == 0 else pu
                    for nb in range(NB):
                        P.op("tensor", lambda e, j=j, m=m, nb=nb, ps=ps, ab=ab,
                             last=(batch_end and m == 1 and nb == NB - 1): dnmm(e, j, m, nb, ps, ab, last))
                if batch_end:
                    v_abdone[(ds, ab)] = s_mm.add(1)
                    if ab >= AB0 and ab + 2 < NAB:
                        emit_down_act(ds, ab + 2)
            def dcor(e, m, nb, ps, last):
                ins = e.matmul(
                    ps[:, nb * 512:(nb + 1) * 512], nzsd[:, m * 128:(m + 1) * 128],
                    Asb[:, nb * 512:(nb + 1) * 512], start=False, stop=True)
                if last:
                    ins.then_inc(s_mm.h, 1)
            for m in range(2):
                ps = pg if m == 0 else pu
                for nb in range(NB):
                    P.op("tensor", lambda e, m=m, nb=nb, ps=ps, last=(m == 1 and nb == NB - 1): dcor(e, m, nb, ps, last))
            v_dsacc[ds] = s_mm.add(1)
            P.wait("scalar", s_mm, v_dsacc[ds])
            P.wait("vector", s_mm, v_dsacc[ds])
            P.wait("vector", s_ld, v_x1sl[ds])
            if ds >= 1:
                P.wait("vector", s_r2, v_r2[ds - 1])
                P.wait("tensor", s_dq, v_evst[(ds - 1, 1)])
                P.wait("scalar", s_mm, v_trv[(ds - 1, 0)])
                P.wait("vector", s_mm, v_trv[(ds - 1, 1)])
            P.op("scalar", lambda e: e.activation(dstg[:], pg[:], AF.Copy).then_inc(s_ac.h, 1))
            v_cp0 = s_ac.add(1)
            P.op("vector", lambda e: e.tensor_copy(dstg2[:], pu[:]).then_inc(s_dq.h, 1))
            v_cp1 = s_dq.add(1)
            v_psfree[ds] = (v_cp0, v_cp1)
            for m in range(2):
                src = dstg if m == 0 else dstg2
                if m == 0:
                    P.wait("tensor", s_ac, v_cp0)
                else:
                    P.wait("tensor", s_dq, v_evst[(ds, 0)])
                    P.wait("tensor", s_dq, v_cp1)
                def dtr(e, tt, src, last):
                    ins = e.transpose(
                        pA[:].bitcast(BF16)[:, tt * 128:(tt + 1) * 128],
                        src[:, tt * 128:(tt + 1) * 128], ident_b[:])
                    if last:
                        ins.then_inc(s_mm.h, 1)
                for tt in range(8):
                    P.op("tensor", lambda e, tt=tt, src=src, last=(tt == 7): dtr(e, tt, src, last))
                v_trv[(ds, m)] = s_mm.add(1)
                P.wait("vector", s_mm, v_trv[(ds, m)])
                def evadd(e, tt, m, last):
                    ins = e.tensor_tensor(
                        out=r2st[:, tt, m * 128:(m + 1) * 128],
                        in0=pA[:].bitcast(BF16)[:, tt * 128:(tt + 1) * 128],
                        in1=x1f[:].rearrange("k (c d) -> k c d", d=DSW)[:, tt, m * 128:(m + 1) * 128],
                        op=ALU.add)
                    if last:
                        ins.then_inc(s_dq.h, 1)
                for tt in range(8):
                    P.op("vector", lambda e, tt=tt, m=m, last=(tt == 7): evadd(e, tt, m, last))
                v_evst[(ds, m)] = s_dq.add(1)
            P.wait("sync", s_dq, v_evst[(ds, 1)])
            P.op("sync", lambda e, ds=ds: e.dma_start(
                out=r2_sp[:, ds * DSW:(ds + 1) * DSW].rearrange("(c k) d -> k c d", k=128),
                in_=r2st[:]).then_inc(s_r2.h, 16))
            v_r2[ds] = s_r2.add(16)

        v_fout = {}
        v_fld = {}
        v_red = {}
        sqsc = [x1f[:], wd[:].bitcast(F32).rearrange("k c d -> k (c d)")[:, 0:D]]
        P.wait("scalar", s_r2, v_r2[NDS - 1])
        for w in range(2):
            for i in range(4):
                tt = w * 4 + i
                fb = fstg[i]
                sfl = s_fl[tt % 2]
                if tt >= 4:
                    P.wait("scalar", s_fo[(tt - 4) % 2], v_fout[tt - 4][1])
                P.op("scalar", lambda e, tt=tt, fb=fb, sfl=sfl: e.dma_start(
                    out=fb[:], in_=r2_sp[tt * 128:(tt + 1) * 128, :]).then_inc(sfl.h, 16))
                v_fld[tt] = (sfl, sfl.add(16))
            for i in range(4):
                tt = w * 4 + i
                fb = fstg[i]
                sq = sqsc[tt % 2]
                P.wait("scalar", v_fld[tt][0], v_fld[tt][1])
                if tt >= 2:
                    P.wait("scalar", s_dq, v_red[tt - 2])
                P.op("scalar", lambda e, fb=fb, sq=sq: e.activation(sq, fb[:], AF.Square).then_inc(s_ac.h, 1))
                v_sqr = s_ac.add(1)
                P.wait("vector", v_fld[tt][0], v_fld[tt][1])
                P.op("vector", lambda e, tt=tt, fb=fb: e.reduce_sum(
                    out=colA[:, tt % 4:tt % 4 + 1], in_=fb[:], axis=AXX))
                P.wait("vector", s_ac, v_sqr)
                P.op("vector", lambda e, tt=tt, sq=sq: e.reduce_sum(
                    out=colB[:, tt % 4:tt % 4 + 1], in_=sq, axis=AXX).then_inc(s_dq.h, 1))
                v_red[tt] = s_dq.add(1)
            P.wait("vector", s_dq, v_red[w * 4 + 3])
            cs = slice(0, 4)

            def fch(fn):
                P.op("vector", lambda e, fn=fn: fn(e).then_inc(s_dq.h, 1))
                v = s_dq.add(1)
                P.wait("vector", s_dq, v)
                return v
            fch(lambda e, cs=cs: e.tensor_scalar(out=colA[:, cs], in0=colA[:, cs], scalar1=1.0 / D, scalar2=None, op0=ALU.mult))
            fch(lambda e, cs=cs: e.tensor_scalar(out=colB[:, cs], in0=colB[:, cs], scalar1=1.0 / D, scalar2=None, op0=ALU.mult))
            fch(lambda e, cs=cs: e.tensor_tensor(out=colC[:, cs], in0=colA[:, cs], in1=colA[:, cs], op=ALU.mult))
            fch(lambda e, cs=cs: e.tensor_tensor(out=colB[:, cs], in0=colB[:, cs], in1=colC[:, cs], op=ALU.subtract))
            v_stats = fch(lambda e, cs=cs: e.tensor_scalar(out=colB[:, cs], in0=colB[:, cs], scalar1=EPS, scalar2=None, op0=ALU.add))
            P.wait("scalar", s_dq, v_stats)
            P.op("scalar", lambda e, cs=cs: e.activation(colB[:, cs], colB[:, cs], AF.Sqrt).then_inc(s_ac.h, 1))
            v_sq = s_ac.add(1)
            P.wait("vector", s_ac, v_sq)
            fch(lambda e, cs=cs: e.reciprocal(colB[:, cs], colB[:, cs]))
            fch(lambda e, cs=cs: e.tensor_tensor(out=colC[:, cs], in0=colA[:, cs], in1=colB[:, cs], op=ALU.mult))
            for i in range(4):
                tt = w * 4 + i
                fb = fstg[i]
                sfo = s_fo[tt % 2]
                P.op("vector", lambda e, tt=tt, fb=fb: e.scalar_tensor_tensor(
                    out=fb[:], in0=fb[:], scalar=colB[:, tt % 4:tt % 4 + 1],
                    in1=colC[:, tt % 4:tt % 4 + 1].broadcast_to([128, D]),
                    op0=ALU.mult, op1=ALU.subtract).then_inc(s_dq.h, 1))
                v_norm = s_dq.add(1)
                P.wait("sync", s_dq, v_norm)
                P.op("sync", lambda e, tt=tt, fb=fb, sfo=sfo: e.dma_start(
                    out=out[tt * 128:(tt + 1) * 128, :], in_=fb[:]).then_inc(sfo.h, 16))
                v_fout[tt] = (sfo, sfo.add(16))
        P.wait("sync", s_fo[0], s_fo[0].v)
        P.wait("sync", s_fo[1], s_fo[1].v)



def emit_attn(nc, P, es, xin, x_own, hsel, W, sems, pers, x1_tile_hook):
    s_ld, s_sc, s_wl, s_mm, s_dq, s_bc, s_ac, s_st, s_gp = (
        sems[k] for k in ("ld", "sc", "wl", "mm", "dq", "bc", "ac", "st", "gp"))
    s_wb = [sems["wb0"], sems["wb1"], sems["wb2"]]
    s_ci, s_co, s_cc = sems["ci"], sems["co"], sems["cc"]
    ident_b = pers["ident_b"]
    ident_f = pers["ident_f"]
    pA, pb0, pb1, pg, pu = pers["pA"], pers["pb0"], pers["pb1"], pers["pg"], pers["pu"]
    iot = pers["iot"]

    at = ExitStack()
    mka = lambda name, shape, dt: at.enter_context(nc.sbuf_tensor(name, shape, dt))
    with at:
        h = mka("h", [128, NTI, T], F32)
        ccstg = mka("ccstg", [128, 8], F32)
        ccres = mka("ccres", [128, 8], F32)
        sinit_re = mka("sinit_re", [128, NTI], F32)
        sinit_im = mka("sinit_im", [128, NTI], F32)
        nsinit_im = mka("nsinit_im", [128, NTI], F32)
        wslbuf = [mka(f"wsl{i}", [128, 16384], BF16) for i in range(2)]
        arena = mka("arena", [128, 24576], F32)
        onesr = mka("onesr", [1, 128], BF16)
        onesc = mka("onesc", [128, 1], BF16)
        onesrf = mka("onesrf", [1, 128], F32)
        mu_r = mka("mu_r", [1, 512], F32)
        rs_r = mka("rs_r", [1, 512], F32)
        NC = NTI * NL
        vq = {q: mka("v_" + q, [128, NC], F32)
              for q in ("l1w", "l1b", "l2w", "l2b", "scl", "b2", "pzr", "pzi", "lcr", "lci")}
        b1v = mka("b1v", [128, NTF * NL], F32)
        binv = mka("binv", [128, NTI], F32)
        lnlw = mka("lnlw", [128, NTI], F32)
        lnlb = mka("lnlb", [128, NTI], F32)
        boutv = mka("boutv", [128, NTV], F32)
        pvre = mka("pvre", [128, NC], F32)
        pvim = mka("pvim", [128, NC], F32)
        nlci = mka("nlci", [128, NC], F32)
        pwre = mka("pwre", [128, NC * 11], F32)
        pwim = mka("pwim", [128, NC * 11], F32)
        pwin = mka("pwin", [128, NC * 11], F32)
        tmpc = [mka(f"tmpc{i}", [128, NC], F32) for i in range(3)]
        hselsb = mka("hselsb", [128, 2], F32)
        colA = mka("colA", [128, 1], F32)
        colB = mka("colB", [128, 1], F32)
        colC = mka("colC", [128, 1], F32)

        def w1T(i):
            return wslbuf[i][:, 0:8192].rearrange("k (c j) -> k c j", j=DFF)

        def w2T(i):
            return wslbuf[i][:, 8192:16384].rearrange("k (c j) -> k c j", j=DI)
        woutT = wslbuf[0][:].rearrange("k (c j) -> k c j", j=DM)

        AB = arena[:].bitcast(BF16)
        y4 = AB[:, 0:4096].rearrange("k (c j) -> k c j", j=1024)
        y3v = AB[:, 0:8192].rearrange("k (c j) -> k c j", j=512)
        sc1 = AB[:, 4096:5120]
        sc2 = AB[:, 5120:6144]
        sc3 = AB[:, 6144:7168]
        pl1f = arena[:, 8192:9216]
        pl2f = arena[:, 9216:10240]
        pl_re4 = AB[:, 8192:12288].rearrange("k (c j) -> k c j", j=1024)
        pl_im4 = AB[:, 12288:16384].rearrange("k (c j) -> k c j", j=1024)
        pl_re4b = AB[:, 24576:28672].rearrange("k (c j) -> k c j", j=1024)
        pl_im4b = AB[:, 28672:32768].rearrange("k (c j) -> k c j", j=1024)
        psetR = lambda li: pl_re4 if li % 2 == 0 else pl_re4b
        psetI = lambda li: pl_im4 if li % 2 == 0 else pl_im4b
        sct = AB[:, 7168:8192]
        y2 = AB[:, 20480:24576].rearrange("k (c j) -> k c j", j=1024)
        hbt = AB[:, 32768:34816]
        sqt = AB[:, 34816:36864]
        u1 = arena[:, 18432:18944]
        u2 = arena[:, 18944:19456]
        wraw = AB[:, 40960:49152]
        xTv = AB[:, 0:16384].rearrange("k (c j) -> k c j", j=512)
        winT = AB[:, 16384:32768].rearrange("k (c j) -> k c j", j=512)
        xstg = AB[:, 32768:36864]
        vstg = arena[0:4, 0:2048]
        bstg = arena[0:32, 2048:2176]
        hl = AB[:, 40960:45056].rearrange("k (c j) -> k c j", j=1024)
        ao_t = AB[:, 0:32768].rearrange("k (c j) -> k c j", j=4096)
        x0st = arena[:, 16384:20480]
        r1f = arena[:, 20480:24576]
        sqf = x0st
        wostg = AB[:, 45056:46080]

        def vch(fn):
            P.op("vector", lambda e, fn=fn: fn(e).then_inc(s_dq.h, 1))
            v = s_dq.add(1)
            P.wait("vector", s_dq, v)
            return v

        P.op("sync", lambda e: e.dma_start(out=hselsb[:], in_=hsel).then_inc(s_ld.h, 16))
        v_hsel = s_ld.add(16)
        P.wait("vector", s_gp, pers["v_iota"])
        vch(lambda e: e.tensor_scalar(out=onesr[:], in0=iot[0:1, :], scalar1=0.0, scalar2=1.0, op0=ALU.mult, op1=ALU.add))
        vch(lambda e: e.tensor_scalar(out=onesc[:], in0=iot[:, 0:1], scalar1=0.0, scalar2=1.0, op0=ALU.mult, op1=ALU.add))
        vch(lambda e: e.tensor_scalar(out=onesrf[:], in0=iot[0:1, :], scalar1=0.0, scalar2=1.0, op0=ALU.mult, op1=ALU.add))

        def load_vec(src, nrows, width, dest):
            P.op("sync", lambda e, src=src, nrows=nrows, width=width: e.dma_start(
                out=vstg[0:nrows, 0:width], in_=src).then_inc(s_ld.h, 16))
            lv = s_ld.add(16)
            P.wait("tensor", s_ld, lv)
            P.wait("tensor", s_dq, pers["v_identf"])
            nch = width // 128
            for c in range(nch):
                P.op("tensor", lambda e, c=c, nrows=nrows: e.transpose(
                    pA[0:128, 0:nrows], vstg[0:nrows, c * 128:(c + 1) * 128],
                    ident_f[0:nrows, 0:nrows]).then_inc(s_mm.h, 1))
                mv = s_mm.add(1)
                P.wait("vector", s_mm, mv)
                P.op("vector", lambda e, c=c, nrows=nrows: e.tensor_copy(
                    dest(c), pA[0:128, 0:nrows]).then_inc(s_dq.h, 1))
                dv = s_dq.add(1)
                P.wait("tensor", s_dq, dv)
            P.wait("sync", s_dq, s_dq.v)

        for q, src in (("l1w", W["ln1_w"]), ("l1b", W["ln1_b"]), ("l2w", W["ln2_w"]),
                       ("l2b", W["ln2_b"]), ("scl", W["sc_lin"]), ("b2", W["b2"]),
                       ("pzr", W["phz_re"]), ("pzi", W["phz_im"]),
                       ("lcr", W["lc_re"]), ("lci", W["lc_im"])):
            load_vec(src, NL, DI, lambda c, q=q: vq[q][:, c * NL:(c + 1) * NL])
        load_vec(W["b1"], NL, DFF, lambda c: b1v[:, c * NL:(c + 1) * NL])
        load_vec(W["bin"].rearrange("(a d) -> a d", a=1), 1, DI, lambda c: binv[:, c:c + 1])
        load_vec(W["lnl_w"].rearrange("(a d) -> a d", a=1), 1, DI, lambda c: lnlw[:, c:c + 1])
        load_vec(W["lnl_b"].rearrange("(a d) -> a d", a=1), 1, DI, lambda c: lnlb[:, c:c + 1])
        P.op("sync", lambda e: e.dma_start(out=bstg[:], in_=W["bout"].rearrange("(a d) -> a d", d=128)).then_inc(s_ld.h, 16))
        lv = s_ld.add(16)
        P.wait("tensor", s_ld, lv)
        P.op("tensor", lambda e: e.transpose(pA[0:128, 0:32], bstg[:], ident_f[0:32, 0:32]).then_inc(s_mm.h, 1))
        mv = s_mm.add(1)
        P.wait("vector", s_mm, mv)
        vch(lambda e: e.tensor_copy(boutv[:], pA[0:128, 0:32]))
        P.wait("tensor", s_dq, s_dq.v)
        P.wait("sync", s_dq, s_dq.v)

        vch(lambda e: e.tensor_tensor(out=tmpc[0][:], in0=vq["pzr"][:], in1=vq["pzr"][:], op=ALU.mult))
        vch(lambda e: e.tensor_tensor(out=tmpc[1][:], in0=vq["pzi"][:], in1=vq["pzi"][:], op=ALU.mult))
        vch(lambda e: e.tensor_tensor(out=tmpc[0][:], in0=tmpc[0][:], in1=tmpc[1][:], op=ALU.add))
        vv = vch(lambda e: e.tensor_scalar(out=tmpc[0][:], in0=tmpc[0][:], scalar1=1e-30, scalar2=None, op0=ALU.max))
        P.wait("scalar", s_dq, vv)
        P.op("scalar", lambda e: e.activation(tmpc[0][:], tmpc[0][:], AF.Sqrt).then_inc(s_ac.h, 1))
        av = s_ac.add(1)
        P.wait("scalar", s_ac, av)
        P.op("scalar", lambda e: e.activation(tmpc[1][:], tmpc[0][:], AF.Tanh).then_inc(s_ac.h, 1))
        av = s_ac.add(1)
        P.wait("vector", s_ac, av)
        vch(lambda e: e.reciprocal(tmpc[2][:], tmpc[0][:]))
        vch(lambda e: e.tensor_tensor(out=tmpc[2][:], in0=tmpc[2][:], in1=tmpc[1][:], op=ALU.mult))
        vch(lambda e: e.tensor_tensor(out=pvre[:], in0=tmpc[2][:], in1=vq["pzr"][:], op=ALU.mult))
        vch(lambda e: e.tensor_tensor(out=pvim[:], in0=tmpc[2][:], in1=vq["pzi"][:], op=ALU.mult))
        vch(lambda e: e.tensor_scalar(out=nlci[:], in0=vq["lci"][:], scalar1=-1.0, scalar2=None, op0=ALU.mult))
        pwr3 = pwre[:].rearrange("k (c l) -> k c l", l=11)
        pwi3 = pwim[:].rearrange("k (c l) -> k c l", l=11)
        vch(lambda e: e.tensor_copy(pwr3[:, :, 0], pvre[:]))
        vch(lambda e: e.tensor_copy(pwi3[:, :, 0], pvim[:]))
        for k in range(10):
            vch(lambda e, k=k: e.tensor_tensor(out=tmpc[0][:], in0=pwr3[:, :, k], in1=pwr3[:, :, k], op=ALU.mult))
            vch(lambda e, k=k: e.tensor_tensor(out=tmpc[1][:], in0=pwi3[:, :, k], in1=pwi3[:, :, k], op=ALU.mult))
            vch(lambda e, k=k: e.tensor_tensor(out=tmpc[2][:], in0=pwr3[:, :, k], in1=pwi3[:, :, k], op=ALU.mult))
            vch(lambda e, k=k: e.tensor_tensor(out=pwr3[:, :, k + 1], in0=tmpc[0][:], in1=tmpc[1][:], op=ALU.subtract))
            vch(lambda e, k=k: e.tensor_scalar(out=pwi3[:, :, k + 1], in0=tmpc[2][:], scalar1=2.0, scalar2=None, op0=ALU.mult))
        v_params = vch(lambda e: e.tensor_scalar(out=pwin[:], in0=pwim[:], scalar1=-1.0, scalar2=None, op0=ALU.mult))

        def pwR(ci, k):
            return pwre[:, ci * 11 + k:ci * 11 + k + 1]

        def pwI(ci, k):
            return pwim[:, ci * 11 + k:ci * 11 + k + 1]

        def pwN(ci, k):
            return pwin[:, ci * 11 + k:ci * 11 + k + 1]

        def tr_block(srcfn, dstfn, n):
            for rnd in range((n + 7) // 8):
                cnt = min(8, n - rnd * 8)

                def trx(e, i, k, last):
                    ins = e.transpose(pA[:].bitcast(BF16)[:, k * 128:(k + 1) * 128], srcfn(i), ident_b[:])
                    if last:
                        ins.then_inc(s_mm.h, 1)
                for k in range(cnt):
                    i = rnd * 8 + k
                    P.op("tensor", lambda e, i=i, k=k, last=(k == cnt - 1): trx(e, i, k, last))
                mv = s_mm.add(1)
                P.wait("vector", s_mm, mv)

                def evx(e, i, k, last):
                    ins = e.tensor_copy(dstfn(i), pA[:].bitcast(BF16)[:, k * 128:(k + 1) * 128])
                    if last:
                        ins.then_inc(s_dq.h, 1)
                for k in range(cnt):
                    i = rnd * 8 + k
                    P.op("vector", lambda e, i=i, k=k, last=(k == cnt - 1): evx(e, i, k, last))
                dv = s_dq.add(1)
                P.wait("tensor", s_dq, dv)
            return s_dq.v

        slotW = [arena[:, 0:4096].rearrange("k (c j) -> k c j", j=512),
                 arena[:, 4096:8192].rearrange("k (c j) -> k c j", j=512)]
        slotX = [arena[:, 16384:20480].rearrange("k (c j) -> k c j", j=512),
                 arena[:, 20480:24576].rearrange("k (c j) -> k c j", j=512)]
        relst = {0: 0, 1: 0}
        P.wait("tensor", s_dq, v_params)
        winTsrc = W["WinT"].rearrange("(c k) m -> k c m", k=128)
        xTsrc = xin.rearrange("(c k) t -> k c t", k=128)
        for g in range(4):
            sl = g % 2
            if relst[sl]:
                P.wait("sync", s_ac, relst[sl])
            P.op("sync", lambda e, g=g, sl=sl: e.dma_start(
                out=slotW[sl][:, :, :], in_=winTsrc[:, g * 8:(g + 1) * 8, :]).then_inc(s_wb[sl].h, 16))
            wv = s_wb[sl].add(16)
            P.wait("scalar", s_wb[sl], wv)
            P.op("scalar", lambda e, g=g, sl=sl: e.activation(
                winT[:, g * 8:(g + 1) * 8, :].rearrange("k c j -> k (c j)"),
                slotW[sl][:, :, :].rearrange("k c j -> k (c j)"), AF.Copy).then_inc(s_ac.h, 1))
            relst[sl] = s_ac.add(1)

        v_hck = {}
        v_wmm = {}
        for ck in range(T // 512):
            for g in range(4):
                sl = g % 2
                if ck > 0 and g == 0:
                    P.wait("scalar", s_mm, v_wmm[ck - 1])
                P.wait("sync", s_ac, relst[sl])
                P.op("sync", lambda e, g=g, sl=sl, ck=ck: e.dma_start(
                    out=slotX[sl][:, :, :],
                    in_=xTsrc[:, g * 8:(g + 1) * 8, ck * 512:(ck + 1) * 512]).then_inc(s_wb[sl].h, 16))
                wv = s_wb[sl].add(16)
                P.wait("scalar", s_wb[sl], wv)
                P.op("scalar", lambda e, g=g, sl=sl: e.activation(
                    xTv[:, g * 8:(g + 1) * 8, :].rearrange("k c j -> k (c j)"),
                    slotX[sl][:, :, :].rearrange("k c j -> k (c j)"), AF.Copy).then_inc(s_ac.h, 1))
                relst[sl] = s_ac.add(1)
            P.wait("tensor", s_ac, relst[(4 - 1) % 2])
            P.wait("tensor", s_ac, relst[0])
            if ck > 0:
                P.wait("tensor", s_dq, v_hck[ck - 1])

            def winmm(e, dt_, c, last):
                ps = [pg[:, 0:512], pg[:, 512:1024], pu[:, 0:512], pu[:, 512:1024]][dt_]
                ins = e.matmul(ps, winT[:, c, dt_ * 128:(dt_ + 1) * 128], xTv[:, c, :],
                               start=(c == 0), stop=(c == 31))
                if last:
                    ins.then_inc(s_mm.h, 1)
            for dt_ in range(NTI):
                for c in range(32):
                    P.op("tensor", lambda e, dt_=dt_, c=c, last=(dt_ == NTI - 1 and c == 31): winmm(e, dt_, c, last))
            mv = s_mm.add(1)
            v_wmm[ck] = mv
            P.wait("vector", s_mm, mv)

            def hev(e, dt_, ck, last):
                ps = [pg[:, 0:512], pg[:, 512:1024], pu[:, 0:512], pu[:, 512:1024]][dt_]
                ins = e.tensor_scalar(out=h[:, dt_, ck * 512:(ck + 1) * 512], in0=ps,
                                      scalar1=binv[:, dt_:dt_ + 1], scalar2=None, op0=ALU.add)
                if last:
                    ins.then_inc(s_dq.h, 1)
            for dt_ in range(NTI):
                P.op("vector", lambda e, dt_=dt_, ck=ck, last=(dt_ == NTI - 1): hev(e, dt_, ck, last))
            v_hck[ck] = s_dq.add(1)
        v_h0 = v_hck[T // 512 - 1]
        dbg = pers.get("dbg")

        def dump_t(tag_idx, gate_val, view_fn):
            if dbg is None:
                return
            P.wait("sync", s_dq, gate_val)
            for dt_ in range(NTI):
                P.op("sync", lambda e, dt_=dt_, tag_idx=tag_idx: e.dma_start(
                    out=dbg[tag_idx, dt_ * 128:(dt_ + 1) * 128, :], in_=view_fn(dt_)).then_inc(s_st.h, 16))
                s_st.add(16)
            for eng in ENGS:
                P.wait(eng, s_st, s_st.v)

        def dump_h(tag_idx, gate_val):
            dump_t(tag_idx, gate_val, lambda dt_: h[:, dt_, :])
        dump_h(0, v_h0)

        wrawA = arena[:, 20480:24576]
        w1src = [W["W1T"][li].rearrange("(c k) f -> k c f", k=128) for li in range(NL)]
        w2src = [W["W2T"][li].rearrange("(c k) d -> k c d", k=128) for li in range(NL)]

        def load_w12(li):
            for r in range(2):
                P.wait("gpsimd", s_ac, relst[1])
                P.op("gpsimd", lambda e, li=li, r=r: e.dma_start(
                    out=wrawA[:].rearrange("k (c f) -> k c f", f=2048),
                    in_=w1src[li][:, r * 2:(r + 1) * 2, :]).then_inc(s_wb[2].h, 16))
                wv = s_wb[2].add(16)
                P.wait("scalar", s_wb[2], wv)
                P.op("scalar", lambda e, li=li, r=r: e.activation(
                    w1T(li % 2)[:, r * 2:(r + 1) * 2, :].rearrange("k c f -> k (c f)"),
                    wrawA[:], AF.Copy).then_inc(s_ac.h, 1))
                relst[1] = s_ac.add(1)
            for r in range(4):
                P.wait("gpsimd", s_ac, relst[1])
                P.op("gpsimd", lambda e, li=li, r=r: e.dma_start(
                    out=wrawA[:, 0:2048].rearrange("k (c d) -> k c d", d=512),
                    in_=w2src[li][:, r * 4:(r + 1) * 4, :]).then_inc(s_wb[2].h, 16))
                wv = s_wb[2].add(16)
                P.wait("scalar", s_wb[2], wv)
                P.op("scalar", lambda e, li=li, r=r: e.activation(
                    w2T(li % 2)[:, r * 4:(r + 1) * 4, :].rearrange("k c d -> k (c d)"),
                    wrawA[:, 0:2048], AF.Copy).then_inc(s_ac.h, 1))
                relst[1] = s_ac.add(1)
            return relst[1]

        v_w12 = {0: load_w12(0)}
        if pers.get("dbg3") is not None:
            P.wait("gpsimd", s_dq, v_w12[0])
            P.op("gpsimd", lambda e: e.dma_start(
                out=pers["dbg3"].rearrange("(c k) f -> k c f", k=128),
                in_=w1T(0)[:]).then_inc(s_wl.h, 16))
            s_wl.add(16)
            for eng in ENGS:
                P.wait(eng, s_wl, s_wl.v)

        def emit_ln(wsel, toks, tokoff, out_fn, hsrc=None):
            hs = h if hsrc is None else hsrc
            for ck in range(toks // 512):
                sl = slice(tokoff + ck * 512, tokoff + (ck + 1) * 512)
                P.wait("vector", s_ac, s_ac.v)
                for dt_ in range(NTI):
                    vch(lambda e, dt_=dt_, sl=sl, hs=hs: e.tensor_copy(hbt[:, dt_ * 512:(dt_ + 1) * 512], hs[:, dt_, sl]))
                vv = s_dq.v
                P.wait("scalar", s_dq, vv)
                P.op("scalar", lambda e: e.activation(sqt[:], hbt[:], AF.Square).then_inc(s_ac.h, 1))
                av = s_ac.add(1)
                P.wait("tensor", s_dq, vv)
                P.wait("tensor", s_ac, av)

                def stm(e, dt_, which, last):
                    src = hbt if which == 0 else sqt
                    dst = pb0[0:1, 0:512] if which == 0 else pb1[0:1, 0:512]
                    ins = e.matmul(dst, onesc[:], src[:, dt_ * 512:(dt_ + 1) * 512],
                                   start=(dt_ == 0), stop=(dt_ == NTI - 1))
                    if last:
                        ins.then_inc(s_mm.h, 1)
                for which in range(2):
                    for dt_ in range(NTI):
                        P.op("tensor", lambda e, dt_=dt_, which=which,
                             last=(which == 1 and dt_ == NTI - 1): stm(e, dt_, which, last))
                mv = s_mm.add(1)
                P.wait("vector", s_mm, mv)
                vch(lambda e: e.tensor_scalar(out=mu_r[:], in0=pb0[0:1, 0:512], scalar1=1.0 / DI, scalar2=None, op0=ALU.mult))
                vch(lambda e: e.scalar_tensor_tensor(out=rs_r[:], in0=mu_r[:], scalar=-1.0, in1=mu_r[:], op0=ALU.mult, op1=ALU.mult))
                vch(lambda e: e.scalar_tensor_tensor(out=rs_r[:], in0=pb1[0:1, 0:512], scalar=1.0 / DI, in1=rs_r[:], op0=ALU.mult, op1=ALU.add))
                vv = vch(lambda e: e.tensor_scalar(out=rs_r[:], in0=rs_r[:], scalar1=EPS, scalar2=None, op0=ALU.add))
                P.wait("scalar", s_dq, vv)
                P.op("scalar", lambda e: e.activation(rs_r[:], rs_r[:], AF.Sqrt).then_inc(s_ac.h, 1))
                av = s_ac.add(1)
                P.wait("vector", s_ac, av)
                vv = vch(lambda e: e.reciprocal(rs_r[:], rs_r[:]))
                P.wait("tensor", s_dq, vv)
                P.op("tensor", lambda e: e.matmul(pb0[:, 0:512], onesrf[:], mu_r[:], start=True, stop=True))
                P.op("tensor", lambda e: e.matmul(pb1[:, 0:512], onesrf[:], rs_r[:], start=True, stop=True).then_inc(s_bc.h, 1))
                bv = s_bc.add(1)
                P.wait("vector", s_bc, bv)
                for dt_ in range(NTI):
                    vch(lambda e, dt_=dt_, sl=sl, hs=hs: e.tensor_tensor(out=u1[:], in0=hs[:, dt_, sl], in1=pb0[:, 0:512], op=ALU.subtract))
                    vv = vch(lambda e, dt_=dt_: e.scalar_tensor_tensor(out=u2[:], in0=u1[:], scalar=wsel(dt_),
                                                                       in1=pb1[:, 0:512], op0=ALU.mult, op1=ALU.mult))
                    out_fn(dt_, ck, vv)
                    P.wait("vector", s_ac, s_ac.v)
                P.wait("tensor", s_dq, s_dq.v)

        def emit_pl1(li, dt_):
            ci = dt_ * NL + li
            vch(lambda e, ci=ci: e.tensor_scalar(out=pl1f[:, 0:1], in0=pvre[:, ci:ci + 1], scalar1=1.0, scalar2=None, op0=ALU.mult))
            vch(lambda e, ci=ci: e.tensor_scalar(out=pl2f[:, 0:1], in0=pvim[:, ci:ci + 1], scalar1=1.0, scalar2=None, op0=ALU.mult))
            m, k = 1, 0
            while m < 1024:
                w = min(m, 1024 - m)
                P.op("vector", lambda e, m=m, w=w, ci=ci, k=k: e.tensor_scalar(
                    out=pl1f[:, m:m + w], in0=pl1f[:, 0:w], scalar1=pwR(ci, k), scalar2=None, op0=ALU.mult))
                vch(lambda e, m=m, w=w, ci=ci, k=k: e.tensor_scalar(
                    out=pl2f[:, m:m + w], in0=pl2f[:, 0:w], scalar1=pwR(ci, k), scalar2=None, op0=ALU.mult))
                P.op("vector", lambda e, m=m, w=w, ci=ci, k=k: e.scalar_tensor_tensor(
                    out=pl1f[:, m:m + w], in0=pl2f[:, 0:w], scalar=pwN(ci, k), in1=pl1f[:, m:m + w],
                    op0=ALU.mult, op1=ALU.add))
                vch(lambda e, m=m, w=w, ci=ci, k=k: e.scalar_tensor_tensor(
                    out=pl2f[:, m:m + w], in0=pl1f[:, 0:w], scalar=pwI(ci, k), in1=pl2f[:, m:m + w],
                    op0=ALU.mult, op1=ALU.add))
                m *= 2
                k += 1
            vch(lambda e, dt_=dt_, li=li: e.tensor_copy(psetR(li)[:, dt_, :], pl1f[:]))
            vch(lambda e, dt_=dt_, li=li: e.tensor_copy(psetI(li)[:, dt_, :], pl2f[:]))

        P.wait("vector", s_dq, v_h0)
        for li in range(NL):
            def y_out(dt_, ck, vv, li=li):
                P.wait("scalar", s_dq, vv)
                P.op("scalar", lambda e, dt_=dt_, ck=ck, li=li: e.activation(
                    y4[:, dt_, ck * 512:(ck + 1) * 512], u2[:], AF.Silu,
                    bias=vq["l1b"][:, dt_ * NL + li:dt_ * NL + li + 1]).then_inc(s_ac.h, 1))
                s_ac.add(1)
            emit_ln(lambda dt_, li=li: vq["l1w"][:, dt_ * NL + li:dt_ * NL + li + 1], T, 0, y_out)
            v_y = s_ac.v
            P.wait("vector", s_ac, v_y)
            if li == 0 and dbg is not None:
                P.wait("sync", s_ac, v_y)
                dump_t(5, s_dq.v, lambda dt_: y4[:, dt_, :])

            for dt_ in range(NTI):
                ci = dt_ * NL + li
                if li == 0:
                    emit_pl1(li, dt_)
                yt = y4[:, dt_, :]
                vch(lambda e, ci=ci, yt=yt: e.scalar_tensor_tensor(
                    out=sc1[:, 1:1024], in0=yt[:, 0:1023], scalar=pwR(ci, 0), in1=yt[:, 1:1024],
                    op0=ALU.mult, op1=ALU.add))
                vch(lambda e, yt=yt: e.tensor_copy(sc1[:, 0:1], yt[:, 0:1]))
                vch(lambda e, ci=ci, yt=yt: e.tensor_scalar(
                    out=sc2[:, 1:1024], in0=yt[:, 0:1023], scalar1=pwI(ci, 0), scalar2=None, op0=ALU.mult))
                vch(lambda e: e.tensor_scalar(out=sc2[:, 0:1], in0=sc1[:, 0:1], scalar1=0.0, scalar2=None, op0=ALU.mult))
                Are, Aim, Bre, Bim = sc1, sc2, yt, sc3
                for k in range(1, 10):
                    m = 1 << k
                    P.op("vector", lambda e, m=m, Are=Are, Bre=Bre: e.tensor_copy(Bre[:, 0:m], Are[:, 0:m]))
                    P.op("vector", lambda e, m=m, Aim=Aim, Bim=Bim: e.tensor_copy(Bim[:, 0:m], Aim[:, 0:m]))
                    P.op("vector", lambda e, m=m, ci=ci, k=k, Are=Are, Bre=Bre: e.tensor_scalar(
                        out=Bre[:, m:1024], in0=Are[:, 0:1024 - m], scalar1=pwR(ci, k), scalar2=None, op0=ALU.mult))
                    vch(lambda e, m=m, ci=ci, k=k, Aim=Aim: e.tensor_scalar(
                        out=sct[:, m:1024], in0=Aim[:, 0:1024 - m], scalar1=pwN(ci, k), scalar2=None, op0=ALU.mult))
                    vch(lambda e, m=m, Are=Are, Bre=Bre: e.tensor_tensor(
                        out=Bre[:, m:1024], in0=Bre[:, m:1024], in1=Are[:, m:1024], op=ALU.add))
                    P.op("vector", lambda e, m=m, Bre=Bre: e.tensor_tensor(
                        out=Bre[:, m:1024], in0=Bre[:, m:1024], in1=sct[:, m:1024], op=ALU.add))
                    P.op("vector", lambda e, m=m, ci=ci, k=k, Aim=Aim, Bim=Bim: e.tensor_scalar(
                        out=Bim[:, m:1024], in0=Aim[:, 0:1024 - m], scalar1=pwR(ci, k), scalar2=None, op0=ALU.mult))
                    vch(lambda e, m=m, ci=ci, k=k, Are=Are: e.tensor_scalar(
                        out=sct[:, m:1024], in0=Are[:, 0:1024 - m], scalar1=pwI(ci, k), scalar2=None, op0=ALU.mult))
                    vch(lambda e, m=m, Aim=Aim, Bim=Bim: e.tensor_tensor(
                        out=Bim[:, m:1024], in0=Bim[:, m:1024], in1=Aim[:, m:1024], op=ALU.add))
                    vch(lambda e, m=m, Bim=Bim: e.tensor_tensor(
                        out=Bim[:, m:1024], in0=Bim[:, m:1024], in1=sct[:, m:1024], op=ALU.add))
                    Are, Aim, Bre, Bim = Bre, Bim, Are, Aim
                assert Are is yt
                vch(lambda e, ci=ci, dt_=dt_, Are=Are: e.scalar_tensor_tensor(
                    out=ccstg[:, dt_:dt_ + 1], in0=vq["lcr"][:, ci:ci + 1], scalar=pwR(ci, 10),
                    in1=Are[:, 1023:1024], op0=ALU.mult, op1=ALU.add))
                vch(lambda e, ci=ci, dt_=dt_: e.scalar_tensor_tensor(
                    out=ccstg[:, dt_:dt_ + 1], in0=nlci[:, ci:ci + 1], scalar=pwI(ci, 10),
                    in1=ccstg[:, dt_:dt_ + 1], op0=ALU.mult, op1=ALU.add))
                vch(lambda e, ci=ci, dt_=dt_, Aim=Aim: e.scalar_tensor_tensor(
                    out=ccstg[:, 4 + dt_:5 + dt_], in0=vq["lcr"][:, ci:ci + 1],
                    scalar=pwI(ci, 10), in1=Aim[:, 1023:1024], op0=ALU.mult, op1=ALU.add))
                vch(lambda e, ci=ci, dt_=dt_: e.scalar_tensor_tensor(
                    out=ccstg[:, 4 + dt_:5 + dt_], in0=nlci[:, ci:ci + 1],
                    scalar=pwR(ci, 10), in1=ccstg[:, 4 + dt_:5 + dt_],
                    op0=ALU.mult, op1=ALU.subtract))

            v_cc_rdy = s_dq.v
            P.wait("sync", s_cc, li)
            P.wait("sync", s_dq, v_cc_rdy)
            P.op("sync", lambda e: e.dma_start(out=pers["cc_in"], in_=ccstg[:]).then_inc(s_ci.h, 16))
            v_ci = s_ci.add(16)
            P.wait("gpsimd", s_ci, v_ci)
            P.wait("gpsimd", s_co, s_co.v)
            P.op("gpsimd", lambda e: e.collective_compute(
                "AllGather", ALU.bypass,
                replica_groups=[[0, 1], [2, 3], [4, 5], [6, 7]],
                ins=[pers["cc_in"].opt()],
                outs=[pers["cc_out"].opt()]).then_inc(s_cc.h))
            v_cc = s_cc.add(1)
            P.wait("sync", s_cc, v_cc)
            P.op("sync", lambda e: e.dma_start(out=ccres[:], in_=pers["cc_out"][0:128, :]).then_inc(s_co.h, 16))
            v_co = s_co.add(16)
            P.wait("vector", s_co, v_co)
            if li == 0:
                P.wait("vector", s_ld, v_hsel)
            for dt_ in range(NTI):
                ci = dt_ * NL + li
                vch(lambda e, dt_=dt_: e.tensor_scalar(
                    out=sinit_re[:, dt_:dt_ + 1], in0=ccres[:, dt_:dt_ + 1],
                    scalar1=hselsb[:, 1:2], scalar2=None, op0=ALU.mult))
                vch(lambda e, ci=ci, dt_=dt_: e.scalar_tensor_tensor(
                    out=sinit_re[:, dt_:dt_ + 1], in0=vq["lcr"][:, ci:ci + 1], scalar=hselsb[:, 0:1],
                    in1=sinit_re[:, dt_:dt_ + 1], op0=ALU.mult, op1=ALU.add))
                vch(lambda e, dt_=dt_: e.tensor_scalar(
                    out=nsinit_im[:, dt_:dt_ + 1], in0=ccres[:, 4 + dt_:5 + dt_],
                    scalar1=hselsb[:, 1:2], scalar2=None, op0=ALU.mult))
                vch(lambda e, ci=ci, dt_=dt_: e.scalar_tensor_tensor(
                    out=nsinit_im[:, dt_:dt_ + 1], in0=nlci[:, ci:ci + 1], scalar=hselsb[:, 0:1],
                    in1=nsinit_im[:, dt_:dt_ + 1], op0=ALU.mult, op1=ALU.add))
            for dt_ in range(NTI):
                ci = dt_ * NL + li
                yt = y4[:, dt_, :]
                vch(lambda e, dt_=dt_, yt=yt, li=li: e.scalar_tensor_tensor(
                    out=yt[:], in0=psetR(li)[:, dt_, :], scalar=sinit_re[:, dt_:dt_ + 1], in1=yt[:],
                    op0=ALU.mult, op1=ALU.add))
                vch(lambda e, dt_=dt_, yt=yt, li=li: e.scalar_tensor_tensor(
                    out=yt[:], in0=psetI(li)[:, dt_, :], scalar=nsinit_im[:, dt_:dt_ + 1], in1=yt[:],
                    op0=ALU.mult, op1=ALU.add))
                vch(lambda e, dt_=dt_, ci=ci, yt=yt: e.scalar_tensor_tensor(
                    out=h[:, dt_, :], in0=yt[:], scalar=vq["scl"][:, ci:ci + 1], in1=h[:, dt_, :],
                    op0=ALU.mult, op1=ALU.add))

            if li == 0 and dbg is not None:
                dump_t(6, s_dq.v, lambda dt_: h[:, dt_, :])
            if li + 1 < NL:
                v_w12[li + 1] = load_w12(li + 1)

            def y2_out(dt_, ck, vv, li=li):
                P.op("vector", lambda e, dt_=dt_, ck=ck, li=li: e.tensor_scalar(
                    out=y2[:, dt_, ck * 512:(ck + 1) * 512], in0=u2[:],
                    scalar1=vq["l2b"][:, dt_ * NL + li:dt_ * NL + li + 1], scalar2=None,
                    op0=ALU.add).then_inc(s_dq.h, 1))
                v = s_dq.add(1)
                P.wait("vector", s_dq, v)
            emit_ln(lambda dt_, li=li: vq["l2w"][:, dt_ * NL + li:dt_ * NL + li + 1], T, 0, y2_out)
            v_y2 = s_dq.v
            P.wait("tensor", s_dq, v_y2)
            if li + 1 < NL:
                for dt2 in range(NTI):
                    emit_pl1(li + 1, dt2)
            if li == 0 and dbg is not None:
                P.wait("gpsimd", s_dq, s_dq.v)
                for dt_ in range(NTI):
                    P.op("gpsimd", lambda e, dt_=dt_: e.dma_start(
                        out=dbg[7, dt_ * 128:(dt_ + 1) * 128, :], in_=y2[:, dt_, :]).then_inc(s_wl.h, 16))
                    s_wl.add(16)
                for eng in ENGS:
                    P.wait(eng, s_wl, s_wl.v)
            P.wait("tensor", s_ac, v_w12[li])

            for ck in range(T // 512):
                sl = slice(ck * 512, (ck + 1) * 512)
                v_ftev = {}
                for ft in range(NTF):
                    bank = pb0[:, 0:512] if ft % 2 == 0 else pb1[:, 0:512]
                    if ft >= 2:
                        P.wait("tensor", s_ac, v_ftev[ft - 2])

                    def w1mm(e, ft, c, bank, last, li=li, sl=sl):
                        ins = e.matmul(bank, w1T(li % 2)[:, c, ft * 128:(ft + 1) * 128], y2[:, c, sl],
                                       start=(c == 0), stop=(c == NTI - 1))
                        if last:
                            ins.then_inc(s_mm.h, 1)
                    dbgfirst = (li == 0 and ck == 0 and ft == 0 and pers.get("dbg4") is not None)
                    for c in range(NTI):
                        if False and dbgfirst and c == 1:
                            P.op("tensor", lambda e: e.matmul(pg[:, 0:512], w1T(0)[:, 0, 0:128], y2[:, 0, 0:512], start=True, stop=True).then_inc(s_mm.h, 1))
                            mvx = s_mm.add(1)
                            P.wait("vector", s_mm, mvx)
                            P.op("vector", lambda e: e.tensor_copy(u2[:], pg[:, 0:512]).then_inc(s_dq.h, 1))
                            s_dq.add(1)
                            P.wait("gpsimd", s_dq, s_dq.v)
                            P.op("gpsimd", lambda e: e.dma_start(out=pers["dbg4"][3], in_=u2[:]).then_inc(s_wl.h, 16))
                            P.op("gpsimd", lambda e: e.dma_start(out=pers["dbg4"][1], in_=y2[:, 0, 0:512]).then_inc(s_wl.h, 16))
                            P.op("gpsimd", lambda e: e.dma_start(out=pers["dbg4"][2], in_=w1T(0)[:, 0, 0:512]).then_inc(s_wl.h, 16))
                            s_wl.add(48)
                            for eng in ENGS:
                                P.wait(eng, s_wl, s_wl.v)
                        P.op("tensor", lambda e, ft=ft, c=c, bank=bank, last=(c == NTI - 1),
                             li=li, sl=sl, f=w1mm: f(e, ft, c, bank, last, li, sl))
                    mv = s_mm.add(1)
                    if li == 0 and ck == 0 and ft == 0 and pers.get("dbg4") is not None:
                        P.wait("vector", s_mm, mv)
                        P.op("vector", lambda e: e.tensor_copy(u1[:], pb0[:, 0:512]).then_inc(s_dq.h, 1))
                        s_dq.add(1)
                        P.wait("gpsimd", s_dq, s_dq.v)
                        P.op("gpsimd", lambda e: e.dma_start(out=pers["dbg4"][0], in_=u1[:]).then_inc(s_wl.h, 16))
                        s_wl.add(16)
                        for eng in ENGS:
                            P.wait(eng, s_wl, s_wl.v)
                    P.wait("scalar", s_mm, mv)
                    P.op("scalar", lambda e, ft=ft, bank=bank, li=li: e.activation(
                        y3v[:, ft, :], bank, AF.Silu, bias=b1v[:, ft * NL + li:ft * NL + li + 1]).then_inc(s_ac.h, 1))
                    v_ftev[ft] = s_ac.add(1)
                    if False:
                        P.wait("gpsimd", s_ac, v_ftev[0])
                        P.op("gpsimd", lambda e: e.dma_start(out=pers["dbg4"][1], in_=y2[:, 0, 0:512]).then_inc(s_wl.h, 16))
                        pass
                        for eng in ENGS:
                            P.wait(eng, s_wl, s_wl.v)
                P.wait("tensor", s_ac, v_ftev[NTF - 1])

                def w2mm(e, dt_, c, last, li=li):
                    ps = [pg[:, 0:512], pg[:, 512:1024], pu[:, 0:512], pu[:, 512:1024]][dt_]
                    ins = e.matmul(ps, w2T(li % 2)[:, c, dt_ * 128:(dt_ + 1) * 128], y3v[:, c, :],
                                   start=(c == 0), stop=(c == NTF - 1))
                    if last:
                        ins.then_inc(s_mm.h, 1)
                for dt_ in range(NTI):
                    for c in range(NTF):
                        P.op("tensor", lambda e, dt_=dt_, c=c, last=(dt_ == NTI - 1 and c == NTF - 1),
                             li=li, f=w2mm: f(e, dt_, c, last, li))
                mv = s_mm.add(1)
                P.wait("vector", s_mm, mv)
                for dt_ in range(NTI):
                    vch(lambda e, dt_=dt_, sl=sl, li=li: e.scalar_tensor_tensor(
                        out=h[:, dt_, sl],
                        in0=[pg[:, 0:512], pg[:, 512:1024], pu[:, 0:512], pu[:, 512:1024]][dt_],
                        scalar=vq["b2"][:, dt_ * NL + li:dt_ * NL + li + 1], in1=h[:, dt_, sl],
                        op0=ALU.add, op1=ALU.add))
                vv = s_dq.v
                P.wait("tensor", s_dq, vv)
                if li == 0 and ck == 0 and pers.get("dbg2") is not None:
                    P.wait("gpsimd", s_dq, vv)
                    P.op("gpsimd", lambda e: e.dma_start(
                        out=pers["dbg2"].rearrange("(c k) t -> k c t", k=128),
                        in_=y3v[:]).then_inc(s_wl.h, 16))
                    s_wl.add(16)
                    for eng in ENGS:
                        P.wait(eng, s_wl, s_wl.v)
            dump_h(1 + li, s_dq.v)

        woutsrc = W["WoutT"]
        for c in range(4):
            P.wait("gpsimd", s_ac, relst[1])
            P.op("gpsimd", lambda e, c=c: e.dma_start(
                out=wrawA[:], in_=woutsrc[c * 128:(c + 1) * 128, :]).then_inc(s_wb[2].h, 16))
            wv = s_wb[2].add(16)
            P.wait("scalar", s_wb[2], wv)
            P.op("scalar", lambda e, c=c: e.activation(
                woutT[:, c, :], wrawA[:], AF.Copy).then_inc(s_ac.h, 1))
            relst[1] = s_ac.add(1)
        P.wait("tensor", s_ac, relst[1])
        P.wait("vector", s_ac, relst[1])

        def hl_out(dt_, ck, vv):
            P.op("vector", lambda e, dt_=dt_, ck=ck: e.tensor_scalar(
                out=hl[:, dt_, ck * 512:(ck + 1) * 512], in0=u2[:],
                scalar1=lnlb[:, dt_:dt_ + 1], scalar2=None, op0=ALU.add).then_inc(s_dq.h, 1))
            v = s_dq.add(1)
            P.wait("vector", s_dq, v)
        emit_ln(lambda dt_: lnlw[:, dt_:dt_ + 1], T, 0, hl_out)
        v_hl = s_dq.v
        P.wait("tensor", s_dq, v_hl)
        if pers.get("dbg") is not None:
            P.wait("gpsimd", s_dq, v_hl)
            for dt_ in range(NTI):
                P.op("gpsimd", lambda e, dt_=dt_: e.dma_start(
                    out=pers["dbg"][5, dt_ * 128:(dt_ + 1) * 128, 0:T], in_=hl[:, dt_, :]).then_inc(s_wl.h, 16))
                s_wl.add(16)
            for eng in ENGS:
                P.wait(eng, s_wl, s_wl.v)

        if pers.get("dbg2") is not None:
            P.wait("gpsimd", s_dq, s_dq.v)
            P.op("gpsimd", lambda e: e.dma_start(
                out=pers["dbg2"].rearrange("(a k) d -> k a d", k=128)[:, 0:4, :],
                in_=woutT[:, 0, 0:2048].rearrange("k (a d) -> k a d", d=512)).then_inc(s_wl.h, 16))
            s_wl.add(16)
            for eng in ENGS:
                P.wait(eng, s_wl, s_wl.v)
        v_aoev = {}
        for vt in range(NTV):
            ps = pg if vt % 2 == 0 else pu
            if vt >= 2:
                P.wait("tensor", s_dq, v_aoev[vt - 2])

            def womm(e, vt, c, nb, ps, last):
                ins = e.matmul(ps[:, nb * 512:(nb + 1) * 512],
                               woutT[:, c, vt * 128:(vt + 1) * 128],
                               hl[:, c, nb * 512:(nb + 1) * 512],
                               start=(c == 0), stop=(c == NTI - 1))
                if last:
                    ins.then_inc(s_mm.h, 1)
            for c in range(NTI):
                for nb in range(2):
                    P.op("tensor", lambda e, vt=vt, c=c, nb=nb, ps=ps,
                         last=(c == NTI - 1 and nb == 1): womm(e, vt, c, nb, ps, last))
            mv = s_mm.add(1)
            P.wait("vector", s_mm, mv)
            P.op("vector", lambda e, vt=vt, ps=ps: e.tensor_scalar(
                out=wostg[:], in0=ps[:], scalar1=boutv[:, vt:vt + 1], scalar2=None,
                op0=ALU.add).then_inc(s_dq.h, 1))
            av = s_dq.add(1)
            P.wait("tensor", s_dq, av)

            def aotr(e, tsub, last):
                ins = e.transpose(pA[:].bitcast(BF16)[:, tsub * 128:(tsub + 1) * 128],
                                  wostg[:].rearrange("k (c j) -> k c j", j=128)[:, tsub, :], ident_b[:])
                if last:
                    ins.then_inc(s_mm.h, 1)
            for tsub in range(8):
                P.op("tensor", lambda e, tsub=tsub, last=(tsub == 7): aotr(e, tsub, last))
            mv = s_mm.add(1)
            P.wait("vector", s_mm, mv)

            def aoev(e, tsub, vt, last):
                ins = e.tensor_copy(ao_t[:, tsub, vt * 128:(vt + 1) * 128],
                                    pA[:].bitcast(BF16)[:, tsub * 128:(tsub + 1) * 128])
                if last:
                    ins.then_inc(s_dq.h, 1)
            for tsub in range(8):
                P.op("vector", lambda e, tsub=tsub, vt=vt, last=(tsub == 7): aoev(e, tsub, vt, last))
            v_aoev[vt] = s_dq.add(1)
            P.wait("tensor", s_dq, v_aoev[vt])
        v_ao = v_aoev[NTV - 1]
        if pers.get("dbg") is not None:
            P.wait("gpsimd", s_dq, v_ao)
            P.op("gpsimd", lambda e: e.dma_start(
                out=pers["dbg"][6, 0:128, 0:2048], in_=ao_t[:, 0, 0:2048]).then_inc(s_wl.h, 16))
            P.op("gpsimd", lambda e: e.dma_start(
                out=pers["dbg"][6, 128:256, 0:2048], in_=ao_t[:, 0, 2048:4096]).then_inc(s_wl.h, 16))
            s_wl.add(32)
            for eng in ENGS:
                P.wait(eng, s_wl, s_wl.v)

        P.wait("sync", s_dq, v_ao)
        for tt in range(8):
            P.op("sync", lambda e, tt=tt: e.dma_start(
                out=x0st[:], in_=x_own[tt * 128:(tt + 1) * 128, :]).then_inc(s_ld.h, 16))
            lv = s_ld.add(16)
            P.wait("vector", s_ld, lv)
            vch(lambda e, tt=tt: e.tensor_tensor(out=r1f[:], in0=ao_t[:, tt, :], in1=x0st[:], op=ALU.add))
            vch(lambda e: e.reduce_sum(out=colA[:], in_=r1f[:], axis=AXX))
            vv = s_dq.v
            P.wait("scalar", s_dq, vv)
            P.op("scalar", lambda e: e.activation(sqf[:], r1f[:], AF.Square).then_inc(s_ac.h, 1))
            av = s_ac.add(1)
            P.wait("vector", s_ac, av)
            vch(lambda e: e.reduce_sum(out=colB[:], in_=sqf[:], axis=AXX))
            vch(lambda e: e.tensor_scalar(out=colA[:], in0=colA[:], scalar1=1.0 / DM, scalar2=None, op0=ALU.mult))
            vch(lambda e: e.scalar_tensor_tensor(out=colC[:], in0=colA[:], scalar=0.0, in1=colA[:], op0=ALU.add, op1=ALU.mult))
            vch(lambda e: e.scalar_tensor_tensor(out=colB[:], in0=colB[:], scalar=1.0 / DM, in1=colC[:], op0=ALU.mult, op1=ALU.subtract))
            vv = vch(lambda e: e.tensor_scalar(out=colB[:], in0=colB[:], scalar1=EPS, scalar2=None, op0=ALU.add))
            P.wait("scalar", s_dq, vv)
            P.op("scalar", lambda e: e.activation(colB[:], colB[:], AF.Sqrt).then_inc(s_ac.h, 1))
            av = s_ac.add(1)
            P.wait("vector", s_ac, av)
            vch(lambda e: e.reciprocal(colB[:], colB[:]))
            vch(lambda e: e.tensor_tensor(out=colC[:], in0=colA[:], in1=colB[:], op=ALU.mult))
            vx = vch(lambda e: e.scalar_tensor_tensor(
                out=r1f[:], in0=r1f[:], scalar=colB[:], in1=colC[:].broadcast_to([128, DM]),
                op0=ALU.mult, op1=ALU.subtract))
            vfree = x1_tile_hook(tt, r1f, vx)
            P.wait("vector", s_dq, vfree) if isinstance(vfree, int) and vfree else None
            P.wait("sync", s_dq, vx)
    return



def build(nc):
    x_own = nc.dram_tensor("x_own", [T, DM], F32, kind="ExternalInput").ap()
    hsel = nc.dram_tensor("hsel", [128, 2], F32, kind="ExternalInput").ap()
    anames = {"bin": [DI], "ln1_w": [NL, DI], "ln1_b": [NL, DI],
              "phz_re": [NL, DI], "phz_im": [NL, DI], "lc_re": [NL, DI], "lc_im": [NL, DI],
              "sc_lin": [NL, DI], "ln2_w": [NL, DI], "ln2_b": [NL, DI],
              "b1": [NL, DFF], "b2": [NL, DI],
              "lnl_w": [DI], "lnl_b": [DI], "bout": [DM]}
    W = {}
    for n, shp in anames.items():
        ap = nc.dram_tensor("L0_attn_" + n, shp, F32, kind="ExternalInput").ap()
        W[n] = ap
    W["WinT"] = nc.dram_tensor("xWinT", [DM, DI], F32, kind="ExternalInput").ap()
    w1t_ap = nc.dram_tensor("xW1T", [NL, DI, DFF], F32, kind="ExternalInput").ap()
    w2t_ap = nc.dram_tensor("xW2T", [NL, DFF, DI], F32, kind="ExternalInput").ap()
    W["W1T"] = [w1t_ap[i] for i in range(NL)]
    W["W2T"] = [w2t_ap[i] for i in range(NL)]
    W["WoutT"] = nc.dram_tensor("xWoutT", [DI, DM], F32, kind="ExternalInput").ap()
    x_ownT = nc.dram_tensor("x_ownT", [DM, T], F32, kind="ExternalInput").ap()
    qw_g = nc.dram_tensor("L0_ffn_qw_gate", [D, H], I32, kind="ExternalInput").ap()
    z_g = nc.dram_tensor("L0_ffn_z_gate", [32, H], I32, kind="ExternalInput").ap()
    s_g = nc.dram_tensor("L0_ffn_s_gate", [32, H], F32, kind="ExternalInput").ap()
    qw_u = nc.dram_tensor("L0_ffn_qw_up", [D, H], I32, kind="ExternalInput").ap()
    z_u = nc.dram_tensor("L0_ffn_z_up", [32, H], I32, kind="ExternalInput").ap()
    s_u = nc.dram_tensor("L0_ffn_s_up", [32, H], F32, kind="ExternalInput").ap()
    qw_d = nc.dram_tensor("L0_ffn_qw_down", [H, D], I32, kind="ExternalInput").ap()
    z_d = nc.dram_tensor("L0_ffn_z_down", [HT, D], I32, kind="ExternalInput").ap()
    s_d = nc.dram_tensor("L0_ffn_s_down", [HT, D], F32, kind="ExternalInput").ap()
    out = nc.dram_tensor("out", [T, D], F32, kind="ExternalOutput").ap()
    x1_sp = nc.dram_tensor("x1_sp", [T, D], F32, kind="Internal").ap()
    cc_in = nc.dram_tensor("cc_in", [128, 8], F32, kind="Internal").ap()
    cc_out = nc.dram_tensor("cc_out", [256, 8], F32, kind="Internal").ap()
    act_sp = nc.dram_tensor("act_sp", [H, T], BF16, kind="Internal").ap()
    r2_sp = nc.dram_tensor("r2_sp", [T, D], F32, kind="Internal").ap()

    es = ExitStack()
    with es:
        mk = lambda name, shape, dt: es.enter_context(nc.sbuf_tensor(name, shape, dt))
        mkp = lambda name, shape, dt: es.enter_context(nc.psum_tensor(name, shape, dt))
        P = Plan()
        sems = {k: Sem(nc, es, "s_" + k) for k in (
            "ld", "sc", "wl", "mm", "dq", "bc", "ac", "st", "gp",
            "wb0", "wb1", "wb2", "sp0", "sp1", "sp2", "ab0", "ab1",
            "wd", "r2", "fo0", "fo1", "fl0", "fl1", "ci", "co", "cc", "ar")}
        pers = {}
        pers["pA"] = mkp("pA", [128, 1024], F32)
        pers["pb0"] = mkp("pb0", [128, 512], F32)
        pers["pb1"] = mkp("pb1", [128, 512], F32)
        pers["pg"] = mkp("pg", [128, 1024], F32)
        pers["pu"] = mkp("pu", [128, 1024], F32)
        pers["ident_b"] = mk("ident_b", [128, 128], BF16)
        pers["ident_f"] = mk("ident_f", [128, 128], F32)
        pers["iot"] = mk("iot", [128, 128], I32)
        P.op("gpsimd", lambda e: e.iota(pers["iot"][:], pattern=[[-1, 128]], base=0, channel_multiplier=1).then_inc(sems["gp"].h, 1))
        pers["v_iota"] = sems["gp"].add(1)
        P.wait("vector", sems["gp"], pers["v_iota"])
        P.op("vector", lambda e: e.tensor_scalar(out=pers["ident_f"][:], in0=pers["iot"][:], scalar1=0.0, scalar2=None, op0=ALU.is_equal))
        P.op("vector", lambda e: e.tensor_scalar(out=pers["ident_b"][:], in0=pers["iot"][:], scalar1=0.0, scalar2=None, op0=ALU.is_equal).then_inc(sems["dq"].h, 1))
        pers["v_identf"] = sems["dq"].add(1)
        pers["v_ident"] = pers["v_identf"]
        pers["cc_in"] = cc_in
        pers["cc_out"] = cc_out

        s_st = sems["st"]
        s_dq = sems["dq"]

        def hook(tt, buf, v_ready):
            P.wait("sync", s_dq, v_ready)
            P.op("sync", lambda e, tt=tt, buf=buf: e.dma_start(
                out=x1_sp[tt * 128:(tt + 1) * 128, :], in_=buf[:]).then_inc(s_st.h, 16))
            sv = s_st.add(16)
            P.wait("vector", s_st, sv)
            return 0

        emit_attn(nc, P, es, x_ownT, x_own, hsel, W, sems, pers, hook)

        for eng in ENGS:
            for sm in sems.values():
                P.wait(eng, sm, sm.v)

        emit_ffn(nc, P, es, x1_sp, qw_g, z_g, s_g, qw_u, z_u, s_u, qw_d, z_d, s_d,
                 out, act_sp, r2_sp, sems, pers)

        block = es.enter_context(nc.Block())
        P.replay(block)
    return nc


_NC_CACHE = {}


def _get_nc():
    if "nc" not in _NC_CACHE:
        nc = bass.Bass("TRN2", target_bir_lowering=False)
        build(nc)
        _NC_CACHE["nc"] = nc
    return _NC_CACHE["nc"]


def kernel(**inputs):
    nc = _get_nc()
    xfull = np.ascontiguousarray(inputs["L0_attn_x"], dtype=np.float32)
    drop = {"L0_attn_x", "L0_attn_Win", "L0_attn_W1", "L0_attn_W2", "L0_attn_Wout"}
    wkeys = [k for k in inputs if k not in drop]
    winT_np = np.ascontiguousarray(np.asarray(inputs["L0_attn_Win"]).T)
    w1T_np = np.ascontiguousarray(np.asarray(inputs["L0_attn_W1"]).transpose(0, 2, 1))
    w2T_np = np.ascontiguousarray(np.asarray(inputs["L0_attn_W2"]).transpose(0, 2, 1))
    woutT_np = np.ascontiguousarray(np.asarray(inputs["L0_attn_Wout"]).T)
    in_maps = []
    for core in range(8):
        b, hf = core // 2, core % 2
        hs = np.zeros((128, 2), np.float32)
        hs[:, hf] = 1.0
        m = {"x_own": np.ascontiguousarray(xfull[b, hf * T:(hf + 1) * T]),
             "x_ownT": np.ascontiguousarray(xfull[b, hf * T:(hf + 1) * T].T),
             "xWinT": winT_np, "xW1T": w1T_np, "xW2T": w2T_np, "xWoutT": woutT_np,
             "hsel": hs}
        for k in wkeys:
            arr = inputs[k]
            if not (isinstance(arr, np.ndarray) and arr.flags.c_contiguous):
                arr = np.ascontiguousarray(arr)
                inputs[k] = arr
            m[k] = arr
        in_maps.append(m)
    res = run_bass_kernel_spmd(nc, in_maps, core_ids=list(range(8)))
    out = np.empty((4, S, DM), dtype=np.float32)
    for core in range(8):
        b, hf = core // 2, core % 2
        out[b, hf * T:(hf + 1) * T] = res.results[core]["out"]
    return out


# version 27 (final)
# speedup vs baseline: 1.0505x; 1.0505x over previous
import numpy as np
import ml_dtypes
from contextlib import ExitStack
import concourse.bass as bass
import concourse.mybir as mybir
from concourse.bass_utils import run_bass_kernel_spmd

F32 = mybir.dt.float32
BF16 = mybir.dt.bfloat16
I32 = mybir.dt.int32
AF = mybir.ActivationFunctionType
ALU = mybir.AluOpType
AXX = mybir.AxisListType.X

S = 2048
T = 1024
DM = 4096
DI = 512
DFF = 2048
NL = 4
NTI = DI // 128
NTF = DFF // 128
NTV = DM // 128
NCK = S // 512
D = DM
H = 11008
DT = D // 128
HT = H // 128
NB = T // 512
DSW = 256
NDS = D // DSW
ABATCH = 4
EPS = 1e-5
ENGS = ("sync", "scalar", "vector", "tensor", "gpsimd")


class Sem:
    def __init__(self, nc, es, name):
        self.h = es.enter_context(nc.semaphore(name))
        self.v = 0

    def add(self, n=1):
        self.v += n
        return self.v


class Plan:
    def __init__(self):
        self.ops = {e: [] for e in ENGS}

    def op(self, eng, fn):
        self.ops[eng].append(fn)

    def wait(self, eng, sem, val):
        if val and val > 0:
            self.ops[eng].append(lambda e, s=sem.h, v=val: e.wait_ge(s, v))

    def replay(self, block):
        for eng in ENGS:
            def mkrun(eng):
                def run(e):
                    for fn in self.ops[eng]:
                        fn(e)
                return run
            getattr(block, eng)(mkrun(eng))


def emit_ffn(nc, P, es, x1, qw_g, z_g, s_g, qw_u, z_u, s_u, qw_d, z_d, s_d,
             out, act_sp, r2_sp, sems, pers):
    mk = lambda name, shape, dt: es.enter_context(nc.sbuf_tensor(name, shape, dt))
    s_ld, s_sc, s_wl, s_mm, s_dq, s_bc, s_ac, s_st, s_gp = (
        sems[k] for k in ("ld", "sc", "wl", "mm", "dq", "bc", "ac", "st", "gp"))
    s_wb = [sems["wb0"], sems["wb1"], sems["wb2"]]
    s_ci, s_co, s_cc = sems["ci"], sems["co"], sems["cc"]
    s_sp = [sems["sp0"], sems["sp1"], sems["sp2"]]
    s_ab = [sems["ab0"], sems["ab1"]]
    s_wd, s_r2 = sems["wd"], sems["r2"]
    s_fo = [sems["fo0"], sems["fo1"]]
    s_fl = [sems["fl0"], sems["fl1"]]
    s_ar = sems["ar"]
    ident_b = pers["ident_b"]
    x1f = mk("x1f", [128, D], F32)
    x1b = mk("x1b", [128, D], BF16)
    Asb = mk("Asb", [HT, T], BF16)
    pA, pb0, pb1, pg, pu = pers["pA"], pers["pb0"], pers["pb1"], pers["pg"], pers["pu"]
    v_ident = pers["v_ident"]

    gu = ExitStack()
    mkg = lambda name, shape, dt: gu.enter_context(nc.sbuf_tensor(name, shape, dt))
    with gu:
        sel32 = mkg("sel32", [32, 32 * 128], BF16)
        ind86 = mkg("ind86", [128, 86 * 86], BF16)
        x1T = mkg("x1T", [128, DT, T], BF16)
        wbuf = [mkg(f"wbuf{i}", [128, DT, 128], BF16) for i in range(3)]
        zb = mkg("zb", [32, 1024], I32)
        sbf_ = mkg("sbf_", [32, 1024], F32)
        nzs8 = {(m, i): mkg(f"nzs8{m}{i}", [32, 1024], BF16) for m in "gu" for i in range(2)}
        s8 = {(m, i): mkg(f"s8{m}{i}", [32, 1024], BF16) for m in "gu" for i in range(2)}
        Xs = mkg("Xs", [32, T], BF16)
        tb = mkg("tb", [128, T], BF16)
        actc = [mkg(f"actc{i}", [128, T], BF16) for i in range(3)]

        P.op("gpsimd", lambda e: e.iota(sel32[:], pattern=[[-1, 32], [0, 128]], base=0, channel_multiplier=1, allow_small_or_imprecise_dtypes=True))
        P.op("gpsimd", lambda e: e.iota(ind86[:], pattern=[[-1, 86], [1, 86]], base=0, channel_multiplier=0, allow_small_or_imprecise_dtypes=True).then_inc(s_gp.h, 1))
        v_io = s_gp.add(1)
        P.wait("vector", s_gp, v_io)
        P.op("vector", lambda e: e.tensor_scalar(out=sel32[:], in0=sel32[:], scalar1=0.0, scalar2=None, op0=ALU.is_equal))
        P.op("vector", lambda e: e.tensor_scalar(out=ind86[:], in0=ind86[:], scalar1=0.0, scalar2=None, op0=ALU.is_equal).then_inc(s_dq.h, 1))
        v_sel = s_dq.add(1)

        v_tileTr = {}
        v_round = 0
        for tt in range(T // 128):
            P.op("sync", lambda e, tt=tt: e.dma_start(out=x1f[:], in_=x1[tt * 128:(tt + 1) * 128, :]).then_inc(s_ld.h, 16))
            lv = s_ld.add(16)
            P.wait("vector", s_ld, lv)
            if tt > 0:
                P.wait("vector", s_mm, v_tileTr[tt - 1])
            P.op("vector", lambda e: e.tensor_copy(x1b[:], x1f[:]).then_inc(s_dq.h, 1))
            dv = s_dq.add(1)
            P.wait("sync", s_dq, dv)
            P.wait("tensor", s_dq, dv)
            if tt == 0:
                P.wait("tensor", s_dq, v_ident)
            for rnd in range(8):
                P.wait("tensor", s_dq, v_round)
                def trk(e, c, k, last):
                    ins = e.transpose(
                        pb0[:].bitcast(BF16)[:, k * 128:(k + 1) * 128],
                        x1b[:, c * 128:(c + 1) * 128], ident_b[:])
                    if last:
                        ins.then_inc(s_mm.h, 1)
                for k in range(4):
                    c = rnd * 4 + k
                    P.op("tensor", lambda e, c=c, k=k, last=(k == 3): trk(e, c, k, last))
                mv = s_mm.add(1)
                P.wait("vector", s_mm, mv)
                def evk(e, c, k, tt, last):
                    ins = e.tensor_copy(
                        x1T[:, c, tt * 128:(tt + 1) * 128],
                        pb0[:].bitcast(BF16)[:, k * 128:(k + 1) * 128])
                    if last:
                        ins.then_inc(s_dq.h, 1)
                for k in range(4):
                    c = rnd * 4 + k
                    P.op("vector", lambda e, c=c, k=k, tt=tt, last=(k == 3): evk(e, c, k, tt, last))
                v_round = s_dq.add(1)
            v_tileTr[tt] = s_mm.v
        v_x1T = v_round

        P.wait("tensor", s_dq, v_x1T)
        P.wait("tensor", s_dq, v_sel)
        def xsmm(e, c, nb, last):
            ins = e.matmul(
                pA[0:32, nb * 512:(nb + 1) * 512],
                ind86[:, c * 86:c * 86 + 32], x1T[:, c, nb * 512:(nb + 1) * 512],
                start=(c == 0), stop=(c == DT - 1))
            if last:
                ins.then_inc(s_mm.h, 1)
        for c in range(DT):
            for nb in range(NB):
                P.op("tensor", lambda e, c=c, nb=nb, last=(c == DT - 1 and nb == NB - 1): xsmm(e, c, nb, last))
        mv = s_mm.add(1)
        P.wait("vector", s_mm, mv)
        P.op("vector", lambda e: e.tensor_copy(Xs[:], pA[0:32, :]).then_inc(s_dq.h, 1))
        v_Xs = s_dq.add(1)

        nsteps = 2 * HT
        v_dma, v_accum, v_silu, v_act, v_spill, v_zs = {}, {}, {}, {}, {}, {}
        v_dqblk, v_bcast = {}, {}
        v_accum_h, v_silu_h, v_act_h = {}, {}, {}
        NBATCH = (HT + 7) // 8

        def mat_of(s):
            return "g" if s % 2 == 0 else "u"

        def emit_zs_batch(b):
            j0 = b * 8 * 128
            w = min(1024, H - j0)
            for m, zt, st_ in (("g", z_g, s_g), ("u", z_u, s_u)):
                P.op("scalar", lambda e, zt=zt, j0=j0, w=w: e.dma_start(out=zb[:, 0:w], in_=zt[:, j0:j0 + w]).then_inc(s_sc.h, 16))
                P.op("scalar", lambda e, st_=st_, j0=j0, w=w: e.dma_start(out=sbf_[:, 0:w], in_=st_[:, j0:j0 + w]).then_inc(s_sc.h, 16))
                scv = s_sc.add(32)
                P.wait("vector", s_sc, scv)
                if b >= 2:
                    P.wait("vector", s_mm, v_accum[2 * (8 * (b - 2) + 7) + 1])
                P.op("vector", lambda e, m=m, b=b, w=w: e.scalar_tensor_tensor(
                    out=nzs8[(m, b % 2)][:, 0:w], in0=zb[:, 0:w], scalar=-1.0, in1=sbf_[:, 0:w],
                    op0=ALU.mult, op1=ALU.mult))
                P.op("vector", lambda e, m=m, b=b, w=w: e.tensor_copy(
                    s8[(m, b % 2)][:, 0:w], sbf_[:, 0:w]).then_inc(s_dq.h, 1))
                v_zs[(m, b)] = s_dq.add(1)
                P.wait("scalar", s_dq, v_zs[(m, b)])

        def emit_dma(s):
            j = s // 2
            qw = qw_g if s % 2 == 0 else qw_u
            buf = wbuf[s % 3]
            if s >= 3:
                P.wait("gpsimd", s_mm, v_accum[s - 3])
            sem = s_wb[s % 3]
            src = qw[:, j * 128:(j + 1) * 128].rearrange("(c k) j -> k c j", k=128)
            for h in range(4):
                P.op("gpsimd", lambda e, src=src, buf=buf, h=h, sem=sem: e.dma_start(
                    out=buf[:, h * 8:(h + 1) * 8, :], in_=src[:, h * 8:(h + 1) * 8, :]).then_inc(sem.h, 16))
            v_dma[s] = (sem, sem.add(64))

        def emit_bcast_dequant(s):
            j = s // 2
            b = j // 8
            sb_t = s8[(mat_of(s), b % 2)]
            buf = wbuf[s % 3]
            P.wait("tensor", s_dq, v_zs[(mat_of(s), b)])
            P.wait("vector", v_dma[s][0], v_dma[s][1])
            for blk in range(8):
                bank = pb0 if blk % 2 == 0 else pb1
                prev = v_dqblk.get((s, blk - 2)) if blk >= 2 else v_dqblk.get((s - 1, 6 + blk))
                assert prev is not None or (s == 0 and blk < 2)
                P.wait("tensor", s_dq, prev)
                def bcmm(e, G, g, bank, sb_t, j, last):
                    ins = e.matmul(
                        bank[:, g * 128:(g + 1) * 128], sel32[:, G * 128:(G + 1) * 128],
                        sb_t[:, (j % 8) * 128:(j % 8 + 1) * 128], start=True, stop=True)
                    if last:
                        ins.then_inc(s_bc.h, 1)
                for g in range(4):
                    G = blk * 4 + g
                    P.op("tensor", lambda e, G=G, g=g, bank=bank, sb_t=sb_t, j=j, last=(g == 3): bcmm(e, G, g, bank, sb_t, j, last))
                bcv = s_bc.add(1)
                P.wait("vector", s_bc, bcv)
                P.op("vector", lambda e, blk=blk, buf=buf, bank=bank: e.tensor_tensor(
                    out=buf[:, blk * 4:(blk + 1) * 4, :].rearrange("k c j -> k (c j)"),
                    in0=buf[:, blk * 4:(blk + 1) * 4, :].rearrange("k c j -> k (c j)"),
                    in1=bank[:], op=ALU.mult).then_inc(s_dq.h, 1))
                v_dqblk[(s, blk)] = s_dq.add(1)

        def emit_accum(s):
            j = s // 2
            buf = wbuf[s % 3]
            ps = pg if s % 2 == 0 else pu
            nz = nzs8[(mat_of(s), (j // 8) % 2)]
            P.wait("tensor", s_dq, v_dqblk[(s, 7)])
            if s == 0:
                P.wait("tensor", s_dq, v_Xs)

            def cormm(e, nb, nz, j, ps):
                e.matmul(ps[:, nb * 512:(nb + 1) * 512], nz[:, (j % 8) * 128:(j % 8 + 1) * 128],
                         Xs[:, nb * 512:(nb + 1) * 512], start=False, stop=True).then_inc(s_mm.h, 1)
            for nb in range(NB):
                if s % 2 == 0 and j > 0:
                    P.wait("tensor", s_ac, v_silu_h[(j - 1, nb)])
                if s % 2 == 1 and j > 0:
                    P.wait("tensor", s_dq, v_act_h[(j - 1, nb)])
                for c in range(DT):
                    P.op("tensor", lambda e, c=c, nb=nb, buf=buf, ps=ps: e.matmul(
                        ps[:, nb * 512:(nb + 1) * 512], buf[:, c, :],
                        x1T[:, c, nb * 512:(nb + 1) * 512], start=(c == 0), stop=False))
                P.op("tensor", lambda e, nb=nb, nz=nz, j=j, ps=ps: cormm(e, nb, nz, j, ps))
                v_accum_h[(s, nb)] = s_mm.add(1)
            v_accum[s] = v_accum_h[(s, NB - 1)]

        def emit_As(j, inc=False):
            ac = actc[j % 3]
            P.wait("tensor", s_dq, v_act[j])

            def asmm(e, j, nb, ac, last):
                ins = e.matmul(
                    pA[0:86, nb * 512:(nb + 1) * 512], ind86[:, j * 86:j * 86 + 86],
                    ac[:, nb * 512:(nb + 1) * 512], start=(j == 0), stop=(j == HT - 1))
                if last:
                    ins.then_inc(s_mm.h, 1)
            for nb in range(NB):
                P.op("tensor", lambda e, j=j, nb=nb, ac=ac, last=(inc and nb == NB - 1): asmm(e, j, nb, ac, last))

        emit_zs_batch(0)
        emit_zs_batch(1)
        for s in range(3):
            emit_dma(s)
        emit_bcast_dequant(0)
        emit_bcast_dequant(1)

        for s in range(nsteps):
            j = s // 2
            if s % 2 == 0 and j % 8 == 0 and j > 0 and (j // 8 + 1) < NBATCH:
                emit_zs_batch(j // 8 + 1)
            if s % 2 == 1 and j >= 1:
                emit_As(j - 1)
            emit_accum(s)
            if s % 2 == 0:
                for nb in range(NB):
                    P.wait("scalar", s_mm, v_accum_h[(s, nb)])
                    if j >= 1:
                        P.wait("scalar", s_dq, v_act_h[(j - 1, nb)])
                    P.op("scalar", lambda e, nb=nb: e.activation(
                        tb[:, nb * 512:(nb + 1) * 512], pg[:, nb * 512:(nb + 1) * 512],
                        AF.Silu).then_inc(s_ac.h, 1))
                    v_silu_h[(j, nb)] = s_ac.add(1)
                v_silu[j] = v_silu_h[(j, NB - 1)]
            else:
                ac = actc[j % 3]
                for nb in range(NB):
                    P.wait("vector", s_mm, v_accum_h[(s, nb)])
                    P.wait("vector", s_ac, v_silu_h[(j, nb)])
                    if j >= 3 and nb == 0:
                        P.wait("vector", s_sp[j % 3], v_spill[j - 3][1])
                    P.op("vector", lambda e, ac=ac, nb=nb: e.tensor_tensor(
                        out=ac[:, nb * 512:(nb + 1) * 512], in0=tb[:, nb * 512:(nb + 1) * 512],
                        in1=pu[:, nb * 512:(nb + 1) * 512], op=ALU.mult).then_inc(s_dq.h, 1))
                    v_act_h[(j, nb)] = s_dq.add(1)
                v_act[j] = v_act_h[(j, NB - 1)]
                P.wait("sync", s_dq, v_act[j])
                sem = s_sp[j % 3]
                P.op("sync", lambda e, j=j, ac=ac, sem=sem: e.dma_start(out=act_sp[j * 128:(j + 1) * 128, :], in_=ac[:]).then_inc(sem.h, 16))
                v_spill[j] = (sem, sem.add(16))
            if s + 3 < nsteps:
                emit_dma(s + 3)
            if s + 2 < nsteps:
                emit_bcast_dequant(s + 2)

        emit_As(HT - 1, inc=True)
        mv = s_mm.add(1)
        P.wait("vector", s_mm, mv)
        P.op("vector", lambda e: e.tensor_copy(Asb[:], pA[0:86, :]).then_inc(s_dq.h, 1))
        v_As = s_dq.add(1)
        v_guDone = [v_accum[nsteps - 1], v_spill[HT - 1], v_spill[HT - 2], v_spill[HT - 3]]

    dn = ExitStack()
    mkd = lambda name, shape, dt: dn.enter_context(nc.sbuf_tensor(name, shape, dt))
    with dn:
        for eng in ENGS:
            P.wait(eng, s_mm, v_guDone[0])
            for sm, vv_ in v_guDone[1:]:
                P.wait(eng, sm, vv_)
            P.wait(eng, s_dq, v_As)
        sel86 = mkd("sel86", [86, 86 * 128], BF16)
        wd = mkd("wd", [128, HT, DSW], BF16)
        actd = [mkd(f"actd{i}", [128, ABATCH, T], BF16) for i in range(2)]
        RACT = 32
        AB0 = RACT // ABATCH
        actr = mkd("actr", [128, RACT, T], BF16)
        zdb = mkd("zdb", [86, DSW], I32)
        sdb = mkd("sdb", [86, DSW], F32)
        nzsd = mkd("nzsd", [86, DSW], BF16)
        sdbb = mkd("sdbb", [86, DSW], BF16)
        dstg = mkd("dstg", [128, T], BF16)
        dstg2 = mkd("dstg2", [128, T], BF16)
        r2st = mkd("r2st", [128, 8, DSW], F32)
        fstg = [actr[:, i * 8:(i + 1) * 8, :].bitcast(F32).rearrange("k c d -> k (c d)") for i in range(4)]
        colA = mkd("fcolA", [128, 8], F32)
        colB = mkd("fcolB", [128, 8], F32)
        colC = mkd("fcolC", [128, 8], F32)
        for g8 in range(8):
            P.op("scalar", lambda e, g8=g8: e.dma_start(
                out=actr[:, g8 * 4:(g8 + 1) * 4, :],
                in_=act_sp[g8 * 4 * 128:(g8 + 1) * 4 * 128, :].rearrange("(c k) t -> k c t", k=128)).then_inc(s_ar.h, 16))
        v_actr = s_ar.add(16 * 8)
        P.op("gpsimd", lambda e: e.iota(sel86[:], pattern=[[-1, 86], [0, 128]], base=0, channel_multiplier=1, allow_small_or_imprecise_dtypes=True).then_inc(s_gp.h, 1))
        v_io = s_gp.add(1)
        P.wait("vector", s_gp, v_io)
        P.op("vector", lambda e: e.tensor_scalar(out=sel86[:], in0=sel86[:], scalar1=0.0, scalar2=None, op0=ALU.is_equal).then_inc(s_dq.h, 1))
        v_sel86 = s_dq.add(1)

        NAB = (HT + ABATCH - 1) // ABATCH
        v_wdma, v_adma, v_abdone, v_dsacc, v_zsd, v_x1sl = {}, {}, {}, {}, {}, {}
        v_dqdj, v_bcd, v_evst, v_r2, v_trv = {}, {}, {}, {}, {}

        def wd_batches(ds):
            src = qw_d[:, ds * DSW:(ds + 1) * DSW].rearrange("(c k) d -> k c d", k=128)
            off = 0
            for bi in range(8):
                n = 11 if bi < 7 else 9
                if ds >= 1:
                    ab_last = (off + n - 1) // ABATCH
                    P.wait("gpsimd", s_mm, v_abdone[(ds - 1, ab_last)])
                P.op("gpsimd", lambda e, src=src, off=off, n=n: e.dma_start(
                    out=wd[:, off:off + n, :], in_=src[:, off:off + n, :]).then_inc(s_wd.h, 16))
                v_wdma[(ds, bi)] = s_wd.add(16)
                off += n

        def wd_ready(ds, j):
            return v_wdma[(ds, min(j // 11, 7))]

        def emit_down_zs(ds):
            P.op("scalar", lambda e, ds=ds: e.dma_start(out=zdb[:], in_=z_d[:, ds * DSW:(ds + 1) * DSW]).then_inc(s_sc.h, 16))
            P.op("scalar", lambda e, ds=ds: e.dma_start(out=sdb[:], in_=s_d[:, ds * DSW:(ds + 1) * DSW]).then_inc(s_sc.h, 16))
            scv = s_sc.add(32)
            P.wait("vector", s_sc, scv)
            if ds >= 1:
                P.wait("vector", s_bc, v_bcd[(ds - 1, HT - 1)])
                P.wait("vector", s_mm, v_dsacc[ds - 1])
            P.op("vector", lambda e: e.scalar_tensor_tensor(out=nzsd[:], in0=zdb[:], scalar=-1.0, in1=sdb[:], op0=ALU.mult, op1=ALU.mult))
            P.op("vector", lambda e: e.tensor_copy(sdbb[:], sdb[:]).then_inc(s_dq.h, 1))
            v_zsd[ds] = s_dq.add(1)
            P.wait("scalar", s_dq, v_zsd[ds])

        def emit_down_x1(ds):
            if ds >= 1:
                P.wait("sync", s_dq, v_evst[(ds - 1, 1)])
            P.op("sync", lambda e, ds=ds: e.dma_start(
                out=x1f[:].rearrange("k (c d) -> k c d", d=DSW)[:, 0:8, :],
                in_=x1[:, ds * DSW:(ds + 1) * DSW].rearrange("(c k) d -> k c d", k=128)).then_inc(s_ld.h, 16))
            v_x1sl[ds] = s_ld.add(16)

        def emit_down_act(ds, ab):
            j0 = ab * ABATCH
            n = min(ABATCH, HT - j0)
            key = (ds, ab - 2) if ab >= AB0 + 2 else ((ds - 1, NAB - 2 + (ab - AB0)) if ds >= 1 else None)
            if key:
                P.wait("scalar", s_mm, v_abdone[key])
            sem = s_ab[ab % 2]
            P.op("scalar", lambda e, j0=j0, n=n, ab=ab, sem=sem: e.dma_start(
                out=actd[ab % 2][:, 0:n, :],
                in_=act_sp[j0 * 128:(j0 + n) * 128, :].rearrange("(c k) t -> k c t", k=128)).then_inc(sem.h, 16))
            v_adma[(ds, ab)] = (sem, sem.add(16))

        v_psfree = {}
        for ds in range(NDS):
            wd_batches(ds)
            emit_down_zs(ds)
            emit_down_x1(ds)
            emit_down_act(ds, AB0)
            emit_down_act(ds, AB0 + 1)
            if ds == 0:
                P.wait("tensor", s_dq, v_sel86)
                P.wait("tensor", s_ar, v_actr)
            def emit_bcdq(j, ds=ds):
                bank = pb0 if j % 2 == 0 else pb1
                P.wait("tensor", s_dq, v_dqdj.get((ds, j - 2)) or (v_dqdj.get((ds - 1, HT - 2 + j)) if ds >= 1 else None))
                if j == 0:
                    P.wait("tensor", s_dq, v_zsd[ds])
                P.op("tensor", lambda e, j=j, bank=bank: e.matmul(
                    bank[:, 0:DSW], sel86[:, j * 128:(j + 1) * 128], sdbb[:],
                    start=True, stop=True).then_inc(s_bc.h, 1))
                v_bcd[(ds, j)] = s_bc.add(1)
                P.wait("vector", s_bc, v_bcd[(ds, j)])
                P.wait("vector", s_wd, wd_ready(ds, j))
                P.op("vector", lambda e, j=j, bank=bank: e.tensor_tensor(
                    out=wd[:, j, :], in0=wd[:, j, :], in1=bank[:, 0:DSW], op=ALU.mult).then_inc(s_dq.h, 1))
                v_dqdj[(ds, j)] = s_dq.add(1)

            emit_bcdq(0)
            for j in range(HT):
                ab = j // ABATCH
                if j + 1 < HT:
                    emit_bcdq(j + 1)
                P.wait("tensor", s_dq, v_dqdj[(ds, j)])
                if j >= RACT:
                    P.wait("tensor", v_adma[(ds, ab)][0], v_adma[(ds, ab)][1])
                if ds > 0 and j == 0:
                    P.wait("tensor", s_ac, v_psfree[ds - 1][0])
                    P.wait("tensor", s_dq, v_psfree[ds - 1][1])
                batch_end = (j % ABATCH == ABATCH - 1 or j == HT - 1)

                def dnmm(e, j, m, nb, ps, ab, last):
                    asrc = (actr[:, j, nb * 512:(nb + 1) * 512] if j < RACT
                            else actd[ab % 2][:, j % ABATCH, nb * 512:(nb + 1) * 512])
                    ins = e.matmul(
                        ps[:, nb * 512:(nb + 1) * 512], wd[:, j, m * 128:(m + 1) * 128],
                        asrc, start=(j == 0), stop=False)
                    if last:
                        ins.then_inc(s_mm.h, 1)
                for m in range(2):
                    ps = pg if m == 0 else pu
                    for nb in range(NB):
                        P.op("tensor", lambda e, j=j, m=m, nb=nb, ps=ps, ab=ab,
                             last=(batch_end and m == 1 and nb == NB - 1): dnmm(e, j, m, nb, ps, ab, last))
                if batch_end:
                    v_abdone[(ds, ab)] = s_mm.add(1)
                    if ab >= AB0 and ab + 2 < NAB:
                        emit_down_act(ds, ab + 2)
            def dcor(e, m, nb, ps, last):
                ins = e.matmul(
                    ps[:, nb * 512:(nb + 1) * 512], nzsd[:, m * 128:(m + 1) * 128],
                    Asb[:, nb * 512:(nb + 1) * 512], start=False, stop=True)
                if last:
                    ins.then_inc(s_mm.h, 1)
            for m in range(2):
                ps = pg if m == 0 else pu
                for nb in range(NB):
                    P.op("tensor", lambda e, m=m, nb=nb, ps=ps, last=(m == 1 and nb == NB - 1): dcor(e, m, nb, ps, last))
            v_dsacc[ds] = s_mm.add(1)
            P.wait("scalar", s_mm, v_dsacc[ds])
            P.wait("vector", s_mm, v_dsacc[ds])
            P.wait("vector", s_ld, v_x1sl[ds])
            if ds >= 1:
                P.wait("vector", s_r2, v_r2[ds - 1])
                P.wait("tensor", s_dq, v_evst[(ds - 1, 1)])
                P.wait("scalar", s_mm, v_trv[(ds - 1, 0)])
                P.wait("vector", s_mm, v_trv[(ds - 1, 1)])
            P.op("scalar", lambda e: e.activation(dstg[:], pg[:], AF.Copy).then_inc(s_ac.h, 1))
            v_cp0 = s_ac.add(1)
            P.op("vector", lambda e: e.tensor_copy(dstg2[:], pu[:]).then_inc(s_dq.h, 1))
            v_cp1 = s_dq.add(1)
            v_psfree[ds] = (v_cp0, v_cp1)
            for m in range(2):
                src = dstg if m == 0 else dstg2
                if m == 0:
                    P.wait("tensor", s_ac, v_cp0)
                else:
                    P.wait("tensor", s_dq, v_evst[(ds, 0)])
                    P.wait("tensor", s_dq, v_cp1)
                def dtr(e, tt, src, last):
                    ins = e.transpose(
                        pA[:].bitcast(BF16)[:, tt * 128:(tt + 1) * 128],
                        src[:, tt * 128:(tt + 1) * 128], ident_b[:])
                    if last:
                        ins.then_inc(s_mm.h, 1)
                for tt in range(8):
                    P.op("tensor", lambda e, tt=tt, src=src, last=(tt == 7): dtr(e, tt, src, last))
                v_trv[(ds, m)] = s_mm.add(1)
                P.wait("vector", s_mm, v_trv[(ds, m)])
                def evadd(e, tt, m, last):
                    ins = e.tensor_tensor(
                        out=r2st[:, tt, m * 128:(m + 1) * 128],
                        in0=pA[:].bitcast(BF16)[:, tt * 128:(tt + 1) * 128],
                        in1=x1f[:].rearrange("k (c d) -> k c d", d=DSW)[:, tt, m * 128:(m + 1) * 128],
                        op=ALU.add)
                    if last:
                        ins.then_inc(s_dq.h, 1)
                for tt in range(8):
                    P.op("vector", lambda e, tt=tt, m=m, last=(tt == 7): evadd(e, tt, m, last))
                v_evst[(ds, m)] = s_dq.add(1)
            P.wait("sync", s_dq, v_evst[(ds, 1)])
            P.op("sync", lambda e, ds=ds: e.dma_start(
                out=r2_sp[:, ds * DSW:(ds + 1) * DSW].rearrange("(c k) d -> k c d", k=128),
                in_=r2st[:]).then_inc(s_r2.h, 16))
            v_r2[ds] = s_r2.add(16)

        v_fout = {}
        v_fld = {}
        v_red = {}
        sqsc = [x1f[:], wd[:].bitcast(F32).rearrange("k c d -> k (c d)")[:, 0:D]]
        P.wait("scalar", s_r2, v_r2[NDS - 1])
        for w in range(2):
            for i in range(4):
                tt = w * 4 + i
                fb = fstg[i]
                sfl = s_fl[tt % 2]
                if tt >= 4:
                    P.wait("scalar", s_fo[(tt - 4) % 2], v_fout[tt - 4][1])
                P.op("scalar", lambda e, tt=tt, fb=fb, sfl=sfl: e.dma_start(
                    out=fb[:], in_=r2_sp[tt * 128:(tt + 1) * 128, :]).then_inc(sfl.h, 16))
                v_fld[tt] = (sfl, sfl.add(16))
            for i in range(4):
                tt = w * 4 + i
                fb = fstg[i]
                sq = sqsc[tt % 2]
                P.wait("scalar", v_fld[tt][0], v_fld[tt][1])
                if tt >= 2:
                    P.wait("scalar", s_dq, v_red[tt - 2])
                P.op("scalar", lambda e, fb=fb, sq=sq: e.activation(sq, fb[:], AF.Square).then_inc(s_ac.h, 1))
                v_sqr = s_ac.add(1)
                P.wait("vector", v_fld[tt][0], v_fld[tt][1])
                P.op("vector", lambda e, tt=tt, fb=fb: e.reduce_sum(
                    out=colA[:, tt % 4:tt % 4 + 1], in_=fb[:], axis=AXX))
                P.wait("vector", s_ac, v_sqr)
                P.op("vector", lambda e, tt=tt, sq=sq: e.reduce_sum(
                    out=colB[:, tt % 4:tt % 4 + 1], in_=sq, axis=AXX).then_inc(s_dq.h, 1))
                v_red[tt] = s_dq.add(1)
            P.wait("vector", s_dq, v_red[w * 4 + 3])
            cs = slice(0, 4)

            def fch(fn):
                P.op("vector", lambda e, fn=fn: fn(e).then_inc(s_dq.h, 1))
                v = s_dq.add(1)
                P.wait("vector", s_dq, v)
                return v
            fch(lambda e, cs=cs: e.tensor_scalar(out=colA[:, cs], in0=colA[:, cs], scalar1=1.0 / D, scalar2=None, op0=ALU.mult))
            fch(lambda e, cs=cs: e.tensor_scalar(out=colB[:, cs], in0=colB[:, cs], scalar1=1.0 / D, scalar2=None, op0=ALU.mult))
            fch(lambda e, cs=cs: e.tensor_tensor(out=colC[:, cs], in0=colA[:, cs], in1=colA[:, cs], op=ALU.mult))
            fch(lambda e, cs=cs: e.tensor_tensor(out=colB[:, cs], in0=colB[:, cs], in1=colC[:, cs], op=ALU.subtract))
            v_stats = fch(lambda e, cs=cs: e.tensor_scalar(out=colB[:, cs], in0=colB[:, cs], scalar1=EPS, scalar2=None, op0=ALU.add))
            P.wait("scalar", s_dq, v_stats)
            P.op("scalar", lambda e, cs=cs: e.activation(colB[:, cs], colB[:, cs], AF.Sqrt).then_inc(s_ac.h, 1))
            v_sq = s_ac.add(1)
            P.wait("vector", s_ac, v_sq)
            fch(lambda e, cs=cs: e.reciprocal(colB[:, cs], colB[:, cs]))
            fch(lambda e, cs=cs: e.tensor_tensor(out=colC[:, cs], in0=colA[:, cs], in1=colB[:, cs], op=ALU.mult))
            for i in range(4):
                tt = w * 4 + i
                fb = fstg[i]
                sfo = s_fo[tt % 2]
                P.op("vector", lambda e, tt=tt, fb=fb: e.scalar_tensor_tensor(
                    out=fb[:], in0=fb[:], scalar=colB[:, tt % 4:tt % 4 + 1],
                    in1=colC[:, tt % 4:tt % 4 + 1].broadcast_to([128, D]),
                    op0=ALU.mult, op1=ALU.subtract).then_inc(s_dq.h, 1))
                v_norm = s_dq.add(1)
                P.wait("sync", s_dq, v_norm)
                P.op("sync", lambda e, tt=tt, fb=fb, sfo=sfo: e.dma_start(
                    out=out[tt * 128:(tt + 1) * 128, :], in_=fb[:]).then_inc(sfo.h, 16))
                v_fout[tt] = (sfo, sfo.add(16))
        P.wait("sync", s_fo[0], s_fo[0].v)
        P.wait("sync", s_fo[1], s_fo[1].v)



def emit_attn(nc, P, es, xin, x_own, hsel, W, sems, pers, x1_tile_hook):
    s_ld, s_sc, s_wl, s_mm, s_dq, s_bc, s_ac, s_st, s_gp = (
        sems[k] for k in ("ld", "sc", "wl", "mm", "dq", "bc", "ac", "st", "gp"))
    s_wb = [sems["wb0"], sems["wb1"], sems["wb2"]]
    s_ci, s_co, s_cc = sems["ci"], sems["co"], sems["cc"]
    ident_b = pers["ident_b"]
    ident_f = pers["ident_f"]
    pA, pb0, pb1, pg, pu = pers["pA"], pers["pb0"], pers["pb1"], pers["pg"], pers["pu"]
    iot = pers["iot"]

    at = ExitStack()
    mka = lambda name, shape, dt: at.enter_context(nc.sbuf_tensor(name, shape, dt))
    with at:
        h = mka("h", [128, NTI, T], F32)
        ccstg = mka("ccstg", [128, 8], F32)
        ccres = mka("ccres", [128, 8], F32)
        sinit_re = mka("sinit_re", [128, NTI], F32)
        sinit_im = mka("sinit_im", [128, NTI], F32)
        nsinit_im = mka("nsinit_im", [128, NTI], F32)
        wslbuf = [mka(f"wsl{i}", [128, 16384], BF16) for i in range(2)]
        arena = mka("arena", [128, 24576], F32)
        onesr = mka("onesr", [1, 128], BF16)
        onesc = mka("onesc", [128, 1], BF16)
        onesrf = mka("onesrf", [1, 128], F32)
        mu_r = mka("mu_r", [1, 512], F32)
        rs_r = mka("rs_r", [1, 512], F32)
        NC = NTI * NL
        vq = {q: mka("v_" + q, [128, NC], F32)
              for q in ("l1w", "l1b", "l2w", "l2b", "scl", "b2", "pzr", "pzi", "lcr", "lci")}
        b1v = mka("b1v", [128, NTF * NL], F32)
        binv = mka("binv", [128, NTI], F32)
        lnlw = mka("lnlw", [128, NTI], F32)
        lnlb = mka("lnlb", [128, NTI], F32)
        boutv = mka("boutv", [128, NTV], F32)
        pvre = mka("pvre", [128, NC], F32)
        pvim = mka("pvim", [128, NC], F32)
        nlci = mka("nlci", [128, NC], F32)
        pwre = mka("pwre", [128, NC * 11], F32)
        pwim = mka("pwim", [128, NC * 11], F32)
        pwin = mka("pwin", [128, NC * 11], F32)
        tmpc = [mka(f"tmpc{i}", [128, NC], F32) for i in range(3)]
        hselsb = mka("hselsb", [128, 2], F32)
        colA = mka("colA", [128, 1], F32)
        colB = mka("colB", [128, 1], F32)
        colC = mka("colC", [128, 1], F32)

        def w1T(i):
            return wslbuf[i][:, 0:8192].rearrange("k (c j) -> k c j", j=DFF)

        def w2T(i):
            return wslbuf[i][:, 8192:16384].rearrange("k (c j) -> k c j", j=DI)
        woutT = wslbuf[0][:].rearrange("k (c j) -> k c j", j=DM)

        AB = arena[:].bitcast(BF16)
        y4 = AB[:, 0:4096].rearrange("k (c j) -> k c j", j=1024)
        y3v = AB[:, 0:8192].rearrange("k (c j) -> k c j", j=512)
        sc1 = AB[:, 4096:5120]
        sc2 = AB[:, 5120:6144]
        sc3 = AB[:, 6144:7168]
        pl1f = arena[:, 8192:9216]
        pl2f = arena[:, 9216:10240]
        pl_re4 = AB[:, 8192:12288].rearrange("k (c j) -> k c j", j=1024)
        pl_im4 = AB[:, 12288:16384].rearrange("k (c j) -> k c j", j=1024)
        pl_re4b = AB[:, 24576:28672].rearrange("k (c j) -> k c j", j=1024)
        pl_im4b = AB[:, 28672:32768].rearrange("k (c j) -> k c j", j=1024)
        psetR = lambda li: pl_re4 if li % 2 == 0 else pl_re4b
        psetI = lambda li: pl_im4 if li % 2 == 0 else pl_im4b
        sct = AB[:, 7168:8192]
        y2 = AB[:, 20480:24576].rearrange("k (c j) -> k c j", j=1024)
        hbt = AB[:, 32768:34816]
        sqt = AB[:, 34816:36864]
        u1 = arena[:, 18432:18944]
        u2 = arena[:, 18944:19456]
        wraw = AB[:, 40960:49152]
        xTv = AB[:, 0:16384].rearrange("k (c j) -> k c j", j=512)
        winT = AB[:, 16384:32768].rearrange("k (c j) -> k c j", j=512)
        xstg = AB[:, 32768:36864]
        vstg = arena[0:4, 0:2048]
        bstg = arena[0:32, 2048:2176]
        hl = AB[:, 40960:45056].rearrange("k (c j) -> k c j", j=1024)
        ao_t = AB[:, 0:32768].rearrange("k (c j) -> k c j", j=4096)
        x0st = arena[:, 16384:20480]
        r1f = arena[:, 20480:24576]
        sqf = x0st
        wostg = AB[:, 45056:46080]

        def vch(fn):
            P.op("vector", lambda e, fn=fn: fn(e).then_inc(s_dq.h, 1))
            v = s_dq.add(1)
            P.wait("vector", s_dq, v)
            return v

        P.op("sync", lambda e: e.dma_start(out=hselsb[:], in_=hsel).then_inc(s_ld.h, 16))
        v_hsel = s_ld.add(16)
        P.wait("vector", s_gp, pers["v_iota"])
        vch(lambda e: e.tensor_scalar(out=onesr[:], in0=iot[0:1, :], scalar1=0.0, scalar2=1.0, op0=ALU.mult, op1=ALU.add))
        vch(lambda e: e.tensor_scalar(out=onesc[:], in0=iot[:, 0:1], scalar1=0.0, scalar2=1.0, op0=ALU.mult, op1=ALU.add))
        vch(lambda e: e.tensor_scalar(out=onesrf[:], in0=iot[0:1, :], scalar1=0.0, scalar2=1.0, op0=ALU.mult, op1=ALU.add))

        def load_vec(src, nrows, width, dest):
            P.op("sync", lambda e, src=src, nrows=nrows, width=width: e.dma_start(
                out=vstg[0:nrows, 0:width], in_=src).then_inc(s_ld.h, 16))
            lv = s_ld.add(16)
            P.wait("tensor", s_ld, lv)
            P.wait("tensor", s_dq, pers["v_identf"])
            nch = width // 128
            for c in range(nch):
                P.op("tensor", lambda e, c=c, nrows=nrows: e.transpose(
                    pA[0:128, 0:nrows], vstg[0:nrows, c * 128:(c + 1) * 128],
                    ident_f[0:nrows, 0:nrows]).then_inc(s_mm.h, 1))
                mv = s_mm.add(1)
                P.wait("vector", s_mm, mv)
                P.op("vector", lambda e, c=c, nrows=nrows: e.tensor_copy(
                    dest(c), pA[0:128, 0:nrows]).then_inc(s_dq.h, 1))
                dv = s_dq.add(1)
                P.wait("tensor", s_dq, dv)
            P.wait("sync", s_dq, s_dq.v)

        for q, src in (("l1w", W["ln1_w"]), ("l1b", W["ln1_b"]), ("l2w", W["ln2_w"]),
                       ("l2b", W["ln2_b"]), ("scl", W["sc_lin"]), ("b2", W["b2"]),
                       ("pzr", W["phz_re"]), ("pzi", W["phz_im"]),
                       ("lcr", W["lc_re"]), ("lci", W["lc_im"])):
            load_vec(src, NL, DI, lambda c, q=q: vq[q][:, c * NL:(c + 1) * NL])
        load_vec(W["b1"], NL, DFF, lambda c: b1v[:, c * NL:(c + 1) * NL])
        load_vec(W["bin"].rearrange("(a d) -> a d", a=1), 1, DI, lambda c: binv[:, c:c + 1])
        load_vec(W["lnl_w"].rearrange("(a d) -> a d", a=1), 1, DI, lambda c: lnlw[:, c:c + 1])
        load_vec(W["lnl_b"].rearrange("(a d) -> a d", a=1), 1, DI, lambda c: lnlb[:, c:c + 1])
        P.op("sync", lambda e: e.dma_start(out=bstg[:], in_=W["bout"].rearrange("(a d) -> a d", d=128)).then_inc(s_ld.h, 16))
        lv = s_ld.add(16)
        P.wait("tensor", s_ld, lv)
        P.op("tensor", lambda e: e.transpose(pA[0:128, 0:32], bstg[:], ident_f[0:32, 0:32]).then_inc(s_mm.h, 1))
        mv = s_mm.add(1)
        P.wait("vector", s_mm, mv)
        vch(lambda e: e.tensor_copy(boutv[:], pA[0:128, 0:32]))
        P.wait("tensor", s_dq, s_dq.v)
        P.wait("sync", s_dq, s_dq.v)

        vch(lambda e: e.tensor_tensor(out=tmpc[0][:], in0=vq["pzr"][:], in1=vq["pzr"][:], op=ALU.mult))
        vch(lambda e: e.tensor_tensor(out=tmpc[1][:], in0=vq["pzi"][:], in1=vq["pzi"][:], op=ALU.mult))
        vch(lambda e: e.tensor_tensor(out=tmpc[0][:], in0=tmpc[0][:], in1=tmpc[1][:], op=ALU.add))
        vv = vch(lambda e: e.tensor_scalar(out=tmpc[0][:], in0=tmpc[0][:], scalar1=1e-30, scalar2=None, op0=ALU.max))
        P.wait("scalar", s_dq, vv)
        P.op("scalar", lambda e: e.activation(tmpc[0][:], tmpc[0][:], AF.Sqrt).then_inc(s_ac.h, 1))
        av = s_ac.add(1)
        P.wait("scalar", s_ac, av)
        P.op("scalar", lambda e: e.activation(tmpc[1][:], tmpc[0][:], AF.Tanh).then_inc(s_ac.h, 1))
        av = s_ac.add(1)
        P.wait("vector", s_ac, av)
        vch(lambda e: e.reciprocal(tmpc[2][:], tmpc[0][:]))
        vch(lambda e: e.tensor_tensor(out=tmpc[2][:], in0=tmpc[2][:], in1=tmpc[1][:], op=ALU.mult))
        vch(lambda e: e.tensor_tensor(out=pvre[:], in0=tmpc[2][:], in1=vq["pzr"][:], op=ALU.mult))
        vch(lambda e: e.tensor_tensor(out=pvim[:], in0=tmpc[2][:], in1=vq["pzi"][:], op=ALU.mult))
        vch(lambda e: e.tensor_scalar(out=nlci[:], in0=vq["lci"][:], scalar1=-1.0, scalar2=None, op0=ALU.mult))
        pwr3 = pwre[:].rearrange("k (c l) -> k c l", l=11)
        pwi3 = pwim[:].rearrange("k (c l) -> k c l", l=11)
        vch(lambda e: e.tensor_copy(pwr3[:, :, 0], pvre[:]))
        vch(lambda e: e.tensor_copy(pwi3[:, :, 0], pvim[:]))
        for k in range(10):
            vch(lambda e, k=k: e.tensor_tensor(out=tmpc[0][:], in0=pwr3[:, :, k], in1=pwr3[:, :, k], op=ALU.mult))
            vch(lambda e, k=k: e.tensor_tensor(out=tmpc[1][:], in0=pwi3[:, :, k], in1=pwi3[:, :, k], op=ALU.mult))
            vch(lambda e, k=k: e.tensor_tensor(out=tmpc[2][:], in0=pwr3[:, :, k], in1=pwi3[:, :, k], op=ALU.mult))
            vch(lambda e, k=k: e.tensor_tensor(out=pwr3[:, :, k + 1], in0=tmpc[0][:], in1=tmpc[1][:], op=ALU.subtract))
            vch(lambda e, k=k: e.tensor_scalar(out=pwi3[:, :, k + 1], in0=tmpc[2][:], scalar1=2.0, scalar2=None, op0=ALU.mult))
        v_params = vch(lambda e: e.tensor_scalar(out=pwin[:], in0=pwim[:], scalar1=-1.0, scalar2=None, op0=ALU.mult))

        def pwR(ci, k):
            return pwre[:, ci * 11 + k:ci * 11 + k + 1]

        def pwI(ci, k):
            return pwim[:, ci * 11 + k:ci * 11 + k + 1]

        def pwN(ci, k):
            return pwin[:, ci * 11 + k:ci * 11 + k + 1]

        def tr_block(srcfn, dstfn, n):
            for rnd in range((n + 7) // 8):
                cnt = min(8, n - rnd * 8)

                def trx(e, i, k, last):
                    ins = e.transpose(pA[:].bitcast(BF16)[:, k * 128:(k + 1) * 128], srcfn(i), ident_b[:])
                    if last:
                        ins.then_inc(s_mm.h, 1)
                for k in range(cnt):
                    i = rnd * 8 + k
                    P.op("tensor", lambda e, i=i, k=k, last=(k == cnt - 1): trx(e, i, k, last))
                mv = s_mm.add(1)
                P.wait("vector", s_mm, mv)

                def evx(e, i, k, last):
                    ins = e.tensor_copy(dstfn(i), pA[:].bitcast(BF16)[:, k * 128:(k + 1) * 128])
                    if last:
                        ins.then_inc(s_dq.h, 1)
                for k in range(cnt):
                    i = rnd * 8 + k
                    P.op("vector", lambda e, i=i, k=k, last=(k == cnt - 1): evx(e, i, k, last))
                dv = s_dq.add(1)
                P.wait("tensor", s_dq, dv)
            return s_dq.v

        slotW = [arena[:, 0:4096].rearrange("k (c j) -> k c j", j=512),
                 arena[:, 4096:8192].rearrange("k (c j) -> k c j", j=512)]
        slotX = [arena[:, 16384:20480].rearrange("k (c j) -> k c j", j=512),
                 arena[:, 20480:24576].rearrange("k (c j) -> k c j", j=512)]
        relst = {0: 0, 1: 0}
        P.wait("tensor", s_dq, v_params)
        winTsrc = W["WinT"].rearrange("(c k) m -> k c m", k=128)
        xTsrc = xin.rearrange("(c k) t -> k c t", k=128)
        for g in range(4):
            sl = g % 2
            if relst[sl]:
                P.wait("sync", s_ac, relst[sl])
            P.op("sync", lambda e, g=g, sl=sl: e.dma_start(
                out=slotW[sl][:, :, :], in_=winTsrc[:, g * 8:(g + 1) * 8, :]).then_inc(s_wb[sl].h, 16))
            wv = s_wb[sl].add(16)
            P.wait("scalar", s_wb[sl], wv)
            P.op("scalar", lambda e, g=g, sl=sl: e.activation(
                winT[:, g * 8:(g + 1) * 8, :].rearrange("k c j -> k (c j)"),
                slotW[sl][:, :, :].rearrange("k c j -> k (c j)"), AF.Copy).then_inc(s_ac.h, 1))
            relst[sl] = s_ac.add(1)

        v_hck = {}
        v_wmm = {}
        for ck in range(T // 512):
            for g in range(4):
                sl = g % 2
                if ck > 0 and g == 0:
                    P.wait("scalar", s_mm, v_wmm[ck - 1])
                P.wait("sync", s_ac, relst[sl])
                P.op("sync", lambda e, g=g, sl=sl, ck=ck: e.dma_start(
                    out=slotX[sl][:, :, :],
                    in_=xTsrc[:, g * 8:(g + 1) * 8, ck * 512:(ck + 1) * 512]).then_inc(s_wb[sl].h, 16))
                wv = s_wb[sl].add(16)
                P.wait("scalar", s_wb[sl], wv)
                P.op("scalar", lambda e, g=g, sl=sl: e.activation(
                    xTv[:, g * 8:(g + 1) * 8, :].rearrange("k c j -> k (c j)"),
                    slotX[sl][:, :, :].rearrange("k c j -> k (c j)"), AF.Copy).then_inc(s_ac.h, 1))
                relst[sl] = s_ac.add(1)
            P.wait("tensor", s_ac, relst[(4 - 1) % 2])
            P.wait("tensor", s_ac, relst[0])
            if ck > 0:
                P.wait("tensor", s_dq, v_hck[ck - 1])

            def winmm(e, dt_, c, last):
                ps = [pg[:, 0:512], pg[:, 512:1024], pu[:, 0:512], pu[:, 512:1024]][dt_]
                ins = e.matmul(ps, winT[:, c, dt_ * 128:(dt_ + 1) * 128], xTv[:, c, :],
                               start=(c == 0), stop=(c == 31))
                if last:
                    ins.then_inc(s_mm.h, 1)
            for dt_ in range(NTI):
                for c in range(32):
                    P.op("tensor", lambda e, dt_=dt_, c=c, last=(dt_ == NTI - 1 and c == 31): winmm(e, dt_, c, last))
            mv = s_mm.add(1)
            v_wmm[ck] = mv
            P.wait("vector", s_mm, mv)

            def hev(e, dt_, ck, last):
                ps = [pg[:, 0:512], pg[:, 512:1024], pu[:, 0:512], pu[:, 512:1024]][dt_]
                ins = e.tensor_scalar(out=h[:, dt_, ck * 512:(ck + 1) * 512], in0=ps,
                                      scalar1=binv[:, dt_:dt_ + 1], scalar2=None, op0=ALU.add)
                if last:
                    ins.then_inc(s_dq.h, 1)
            for dt_ in range(NTI):
                P.op("vector", lambda e, dt_=dt_, ck=ck, last=(dt_ == NTI - 1): hev(e, dt_, ck, last))
            v_hck[ck] = s_dq.add(1)
        v_h0 = v_hck[T // 512 - 1]
        dbg = pers.get("dbg")

        def dump_t(tag_idx, gate_val, view_fn):
            if dbg is None:
                return
            P.wait("sync", s_dq, gate_val)
            for dt_ in range(NTI):
                P.op("sync", lambda e, dt_=dt_, tag_idx=tag_idx: e.dma_start(
                    out=dbg[tag_idx, dt_ * 128:(dt_ + 1) * 128, :], in_=view_fn(dt_)).then_inc(s_st.h, 16))
                s_st.add(16)
            for eng in ENGS:
                P.wait(eng, s_st, s_st.v)

        def dump_h(tag_idx, gate_val):
            dump_t(tag_idx, gate_val, lambda dt_: h[:, dt_, :])
        dump_h(0, v_h0)

        wrawA = arena[:, 20480:24576]
        w1src = [W["W1T"][li].rearrange("(c k) f -> k c f", k=128) for li in range(NL)]
        w2src = [W["W2T"][li].rearrange("(c k) d -> k c d", k=128) for li in range(NL)]

        def load_w12(li):
            for r in range(2):
                P.wait("gpsimd", s_ac, relst[1])
                P.op("gpsimd", lambda e, li=li, r=r: e.dma_start(
                    out=wrawA[:].rearrange("k (c f) -> k c f", f=2048),
                    in_=w1src[li][:, r * 2:(r + 1) * 2, :]).then_inc(s_wb[2].h, 16))
                wv = s_wb[2].add(16)
                P.wait("scalar", s_wb[2], wv)
                P.op("scalar", lambda e, li=li, r=r: e.activation(
                    w1T(li % 2)[:, r * 2:(r + 1) * 2, :].rearrange("k c f -> k (c f)"),
                    wrawA[:], AF.Copy).then_inc(s_ac.h, 1))
                relst[1] = s_ac.add(1)
            for r in range(4):
                P.wait("gpsimd", s_ac, relst[1])
                P.op("gpsimd", lambda e, li=li, r=r: e.dma_start(
                    out=wrawA[:, 0:2048].rearrange("k (c d) -> k c d", d=512),
                    in_=w2src[li][:, r * 4:(r + 1) * 4, :]).then_inc(s_wb[2].h, 16))
                wv = s_wb[2].add(16)
                P.wait("scalar", s_wb[2], wv)
                P.op("scalar", lambda e, li=li, r=r: e.activation(
                    w2T(li % 2)[:, r * 4:(r + 1) * 4, :].rearrange("k c d -> k (c d)"),
                    wrawA[:, 0:2048], AF.Copy).then_inc(s_ac.h, 1))
                relst[1] = s_ac.add(1)
            return relst[1]

        v_w12 = {0: load_w12(0)}
        if pers.get("dbg3") is not None:
            P.wait("gpsimd", s_dq, v_w12[0])
            P.op("gpsimd", lambda e: e.dma_start(
                out=pers["dbg3"].rearrange("(c k) f -> k c f", k=128),
                in_=w1T(0)[:]).then_inc(s_wl.h, 16))
            s_wl.add(16)
            for eng in ENGS:
                P.wait(eng, s_wl, s_wl.v)

        def emit_ln(wsel, toks, tokoff, out_fn, hsrc=None):
            hs = h if hsrc is None else hsrc
            for ck in range(toks // 512):
                sl = slice(tokoff + ck * 512, tokoff + (ck + 1) * 512)
                P.wait("vector", s_ac, s_ac.v)
                for dt_ in range(NTI):
                    vch(lambda e, dt_=dt_, sl=sl, hs=hs: e.tensor_copy(hbt[:, dt_ * 512:(dt_ + 1) * 512], hs[:, dt_, sl]))
                vv = s_dq.v
                P.wait("scalar", s_dq, vv)
                P.op("scalar", lambda e: e.activation(sqt[:], hbt[:], AF.Square).then_inc(s_ac.h, 1))
                av = s_ac.add(1)
                P.wait("tensor", s_dq, vv)
                P.wait("tensor", s_ac, av)

                def stm(e, dt_, which, last):
                    src = hbt if which == 0 else sqt
                    dst = pb0[0:1, 0:512] if which == 0 else pb1[0:1, 0:512]
                    ins = e.matmul(dst, onesc[:], src[:, dt_ * 512:(dt_ + 1) * 512],
                                   start=(dt_ == 0), stop=(dt_ == NTI - 1))
                    if last:
                        ins.then_inc(s_mm.h, 1)
                for which in range(2):
                    for dt_ in range(NTI):
                        P.op("tensor", lambda e, dt_=dt_, which=which,
                             last=(which == 1 and dt_ == NTI - 1): stm(e, dt_, which, last))
                mv = s_mm.add(1)
                P.wait("vector", s_mm, mv)
                vch(lambda e: e.tensor_scalar(out=mu_r[:], in0=pb0[0:1, 0:512], scalar1=1.0 / DI, scalar2=None, op0=ALU.mult))
                vch(lambda e: e.scalar_tensor_tensor(out=rs_r[:], in0=mu_r[:], scalar=-1.0, in1=mu_r[:], op0=ALU.mult, op1=ALU.mult))
                vch(lambda e: e.scalar_tensor_tensor(out=rs_r[:], in0=pb1[0:1, 0:512], scalar=1.0 / DI, in1=rs_r[:], op0=ALU.mult, op1=ALU.add))
                vv = vch(lambda e: e.tensor_scalar(out=rs_r[:], in0=rs_r[:], scalar1=EPS, scalar2=None, op0=ALU.add))
                P.wait("scalar", s_dq, vv)
                P.op("scalar", lambda e: e.activation(rs_r[:], rs_r[:], AF.Sqrt).then_inc(s_ac.h, 1))
                av = s_ac.add(1)
                P.wait("vector", s_ac, av)
                vv = vch(lambda e: e.reciprocal(rs_r[:], rs_r[:]))
                P.wait("tensor", s_dq, vv)
                P.op("tensor", lambda e: e.matmul(pb0[:, 0:512], onesrf[:], mu_r[:], start=True, stop=True))
                P.op("tensor", lambda e: e.matmul(pb1[:, 0:512], onesrf[:], rs_r[:], start=True, stop=True).then_inc(s_bc.h, 1))
                bv = s_bc.add(1)
                P.wait("vector", s_bc, bv)
                for dt_ in range(NTI):
                    vch(lambda e, dt_=dt_, sl=sl, hs=hs: e.tensor_tensor(out=u1[:], in0=hs[:, dt_, sl], in1=pb0[:, 0:512], op=ALU.subtract))
                    vv = vch(lambda e, dt_=dt_: e.scalar_tensor_tensor(out=u2[:], in0=u1[:], scalar=wsel(dt_),
                                                                       in1=pb1[:, 0:512], op0=ALU.mult, op1=ALU.mult))
                    out_fn(dt_, ck, vv)
                    P.wait("vector", s_ac, s_ac.v)
                P.wait("tensor", s_dq, s_dq.v)

        def emit_pl1(li, dt_):
            ci = dt_ * NL + li
            vch(lambda e, ci=ci: e.tensor_scalar(out=pl1f[:, 0:1], in0=pvre[:, ci:ci + 1], scalar1=1.0, scalar2=None, op0=ALU.mult))
            vch(lambda e, ci=ci: e.tensor_scalar(out=pl2f[:, 0:1], in0=pvim[:, ci:ci + 1], scalar1=1.0, scalar2=None, op0=ALU.mult))
            m, k = 1, 0
            while m < 1024:
                w = min(m, 1024 - m)
                P.op("vector", lambda e, m=m, w=w, ci=ci, k=k: e.tensor_scalar(
                    out=pl1f[:, m:m + w], in0=pl1f[:, 0:w], scalar1=pwR(ci, k), scalar2=None, op0=ALU.mult))
                vch(lambda e, m=m, w=w, ci=ci, k=k: e.tensor_scalar(
                    out=pl2f[:, m:m + w], in0=pl2f[:, 0:w], scalar1=pwR(ci, k), scalar2=None, op0=ALU.mult))
                P.op("vector", lambda e, m=m, w=w, ci=ci, k=k: e.scalar_tensor_tensor(
                    out=pl1f[:, m:m + w], in0=pl2f[:, 0:w], scalar=pwN(ci, k), in1=pl1f[:, m:m + w],
                    op0=ALU.mult, op1=ALU.add))
                vch(lambda e, m=m, w=w, ci=ci, k=k: e.scalar_tensor_tensor(
                    out=pl2f[:, m:m + w], in0=pl1f[:, 0:w], scalar=pwI(ci, k), in1=pl2f[:, m:m + w],
                    op0=ALU.mult, op1=ALU.add))
                m *= 2
                k += 1
            vch(lambda e, dt_=dt_, li=li: e.tensor_copy(psetR(li)[:, dt_, :], pl1f[:]))
            vch(lambda e, dt_=dt_, li=li: e.tensor_copy(psetI(li)[:, dt_, :], pl2f[:]))

        P.wait("vector", s_dq, v_h0)
        for li in range(NL):
            def y_out(dt_, ck, vv, li=li):
                P.wait("scalar", s_dq, vv)
                P.op("scalar", lambda e, dt_=dt_, ck=ck, li=li: e.activation(
                    y4[:, dt_, ck * 512:(ck + 1) * 512], u2[:], AF.Silu,
                    bias=vq["l1b"][:, dt_ * NL + li:dt_ * NL + li + 1]).then_inc(s_ac.h, 1))
                s_ac.add(1)
            emit_ln(lambda dt_, li=li: vq["l1w"][:, dt_ * NL + li:dt_ * NL + li + 1], T, 0, y_out)
            v_y = s_ac.v
            P.wait("vector", s_ac, v_y)
            if li == 0 and dbg is not None:
                P.wait("sync", s_ac, v_y)
                dump_t(5, s_dq.v, lambda dt_: y4[:, dt_, :])

            for dt_ in range(NTI):
                ci = dt_ * NL + li
                if li == 0:
                    emit_pl1(li, dt_)
                yt = y4[:, dt_, :]
                vch(lambda e, ci=ci, yt=yt: e.scalar_tensor_tensor(
                    out=sc1[:, 1:1024], in0=yt[:, 0:1023], scalar=pwR(ci, 0), in1=yt[:, 1:1024],
                    op0=ALU.mult, op1=ALU.add))
                vch(lambda e, yt=yt: e.tensor_copy(sc1[:, 0:1], yt[:, 0:1]))
                vch(lambda e, ci=ci, yt=yt: e.tensor_scalar(
                    out=sc2[:, 1:1024], in0=yt[:, 0:1023], scalar1=pwI(ci, 0), scalar2=None, op0=ALU.mult))
                vch(lambda e: e.tensor_scalar(out=sc2[:, 0:1], in0=sc1[:, 0:1], scalar1=0.0, scalar2=None, op0=ALU.mult))
                Are, Aim, Bre, Bim = sc1, sc2, yt, sc3
                sct2 = AB[:, 36864:37888]

                def vop(fn):
                    P.op("vector", lambda e, fn=fn: fn(e).then_inc(s_dq.h, 1))
                    return s_dq.add(1)
                v_lv = s_dq.v
                for k in range(1, 10):
                    m = 1 << k
                    P.wait("vector", s_dq, v_lv)
                    v3 = vop(lambda e, m=m, ci=ci, k=k, Are=Are, Bre=Bre: e.tensor_scalar(
                        out=Bre[:, m:1024], in0=Are[:, 0:1024 - m], scalar1=pwR(ci, k), scalar2=None, op0=ALU.mult))
                    v7 = vop(lambda e, m=m, ci=ci, k=k, Aim=Aim, Bim=Bim: e.tensor_scalar(
                        out=Bim[:, m:1024], in0=Aim[:, 0:1024 - m], scalar1=pwR(ci, k), scalar2=None, op0=ALU.mult))
                    vop(lambda e, m=m, ci=ci, k=k, Aim=Aim: e.tensor_scalar(
                        out=sct[:, m:1024], in0=Aim[:, 0:1024 - m], scalar1=pwN(ci, k), scalar2=None, op0=ALU.mult))
                    vop(lambda e, m=m, ci=ci, k=k, Are=Are, sct2=sct2: e.tensor_scalar(
                        out=sct2[:, m:1024], in0=Are[:, 0:1024 - m], scalar1=pwI(ci, k), scalar2=None, op0=ALU.mult))
                    P.wait("vector", s_dq, v3)
                    v5 = vop(lambda e, m=m, Are=Are, Bre=Bre: e.tensor_tensor(
                        out=Bre[:, m:1024], in0=Bre[:, m:1024], in1=Are[:, m:1024], op=ALU.add))
                    P.wait("vector", s_dq, v7)
                    v9 = vop(lambda e, m=m, Aim=Aim, Bim=Bim: e.tensor_tensor(
                        out=Bim[:, m:1024], in0=Bim[:, m:1024], in1=Aim[:, m:1024], op=ALU.add))
                    P.wait("vector", s_dq, v5)
                    vop(lambda e, m=m, Bre=Bre: e.tensor_tensor(
                        out=Bre[:, m:1024], in0=Bre[:, m:1024], in1=sct[:, m:1024], op=ALU.add))
                    P.wait("vector", s_dq, v9)
                    vop(lambda e, m=m, Bim=Bim, sct2=sct2: e.tensor_tensor(
                        out=Bim[:, m:1024], in0=Bim[:, m:1024], in1=sct2[:, m:1024], op=ALU.add))
                    vop(lambda e, m=m, Are=Are, Bre=Bre: e.tensor_copy(Bre[:, 0:m], Are[:, 0:m]))
                    v_lv = vop(lambda e, m=m, Aim=Aim, Bim=Bim: e.tensor_copy(Bim[:, 0:m], Aim[:, 0:m]))
                    Are, Aim, Bre, Bim = Bre, Bim, Are, Aim
                P.wait("vector", s_dq, v_lv)
                assert Are is yt
                vch(lambda e, ci=ci, dt_=dt_, Are=Are: e.scalar_tensor_tensor(
                    out=ccstg[:, dt_:dt_ + 1], in0=vq["lcr"][:, ci:ci + 1], scalar=pwR(ci, 10),
                    in1=Are[:, 1023:1024], op0=ALU.mult, op1=ALU.add))
                vch(lambda e, ci=ci, dt_=dt_: e.scalar_tensor_tensor(
                    out=ccstg[:, dt_:dt_ + 1], in0=nlci[:, ci:ci + 1], scalar=pwI(ci, 10),
                    in1=ccstg[:, dt_:dt_ + 1], op0=ALU.mult, op1=ALU.add))
                vch(lambda e, ci=ci, dt_=dt_, Aim=Aim: e.scalar_tensor_tensor(
                    out=ccstg[:, 4 + dt_:5 + dt_], in0=vq["lcr"][:, ci:ci + 1],
                    scalar=pwI(ci, 10), in1=Aim[:, 1023:1024], op0=ALU.mult, op1=ALU.add))
                vch(lambda e, ci=ci, dt_=dt_: e.scalar_tensor_tensor(
                    out=ccstg[:, 4 + dt_:5 + dt_], in0=nlci[:, ci:ci + 1],
                    scalar=pwR(ci, 10), in1=ccstg[:, 4 + dt_:5 + dt_],
                    op0=ALU.mult, op1=ALU.subtract))

            v_cc_rdy = s_dq.v
            P.wait("sync", s_cc, li)
            P.wait("sync", s_dq, v_cc_rdy)
            P.op("sync", lambda e: e.dma_start(out=pers["cc_in"], in_=ccstg[:]).then_inc(s_ci.h, 16))
            v_ci = s_ci.add(16)
            P.wait("gpsimd", s_ci, v_ci)
            P.wait("gpsimd", s_co, s_co.v)
            P.op("gpsimd", lambda e: e.collective_compute(
                "AllGather", ALU.bypass,
                replica_groups=[[0, 1], [2, 3], [4, 5], [6, 7]],
                ins=[pers["cc_in"].opt()],
                outs=[pers["cc_out"].opt()]).then_inc(s_cc.h))
            v_cc = s_cc.add(1)
            P.wait("sync", s_cc, v_cc)
            P.op("sync", lambda e: e.dma_start(out=ccres[:], in_=pers["cc_out"][0:128, :]).then_inc(s_co.h, 16))
            v_co = s_co.add(16)
            P.wait("vector", s_co, v_co)
            if li == 0:
                P.wait("vector", s_ld, s_ld.v)
            for dt_ in range(NTI):
                ci = dt_ * NL + li
                vch(lambda e, dt_=dt_: e.tensor_scalar(
                    out=sinit_re[:, dt_:dt_ + 1], in0=ccres[:, dt_:dt_ + 1],
                    scalar1=hselsb[:, 1:2], scalar2=None, op0=ALU.mult))
                vch(lambda e, ci=ci, dt_=dt_: e.scalar_tensor_tensor(
                    out=sinit_re[:, dt_:dt_ + 1], in0=vq["lcr"][:, ci:ci + 1], scalar=hselsb[:, 0:1],
                    in1=sinit_re[:, dt_:dt_ + 1], op0=ALU.mult, op1=ALU.add))
                vch(lambda e, dt_=dt_: e.tensor_scalar(
                    out=nsinit_im[:, dt_:dt_ + 1], in0=ccres[:, 4 + dt_:5 + dt_],
                    scalar1=hselsb[:, 1:2], scalar2=None, op0=ALU.mult))
                vch(lambda e, ci=ci, dt_=dt_: e.scalar_tensor_tensor(
                    out=nsinit_im[:, dt_:dt_ + 1], in0=nlci[:, ci:ci + 1], scalar=hselsb[:, 0:1],
                    in1=nsinit_im[:, dt_:dt_ + 1], op0=ALU.mult, op1=ALU.add))
            for dt_ in range(NTI):
                ci = dt_ * NL + li
                yt = y4[:, dt_, :]
                vch(lambda e, dt_=dt_, yt=yt, li=li: e.scalar_tensor_tensor(
                    out=yt[:], in0=psetR(li)[:, dt_, :], scalar=sinit_re[:, dt_:dt_ + 1], in1=yt[:],
                    op0=ALU.mult, op1=ALU.add))
                vch(lambda e, dt_=dt_, yt=yt, li=li: e.scalar_tensor_tensor(
                    out=yt[:], in0=psetI(li)[:, dt_, :], scalar=nsinit_im[:, dt_:dt_ + 1], in1=yt[:],
                    op0=ALU.mult, op1=ALU.add))
                vch(lambda e, dt_=dt_, ci=ci, yt=yt: e.scalar_tensor_tensor(
                    out=h[:, dt_, :], in0=yt[:], scalar=vq["scl"][:, ci:ci + 1], in1=h[:, dt_, :],
                    op0=ALU.mult, op1=ALU.add))

            if li == 0 and dbg is not None:
                dump_t(6, s_dq.v, lambda dt_: h[:, dt_, :])
            if li + 1 < NL:
                v_w12[li + 1] = load_w12(li + 1)

            def y2_out(dt_, ck, vv, li=li):
                P.op("vector", lambda e, dt_=dt_, ck=ck, li=li: e.tensor_scalar(
                    out=y2[:, dt_, ck * 512:(ck + 1) * 512], in0=u2[:],
                    scalar1=vq["l2b"][:, dt_ * NL + li:dt_ * NL + li + 1], scalar2=None,
                    op0=ALU.add).then_inc(s_dq.h, 1))
                v = s_dq.add(1)
                P.wait("vector", s_dq, v)
            emit_ln(lambda dt_, li=li: vq["l2w"][:, dt_ * NL + li:dt_ * NL + li + 1], T, 0, y2_out)
            v_y2 = s_dq.v
            P.wait("tensor", s_dq, v_y2)
            if li + 1 < NL:
                for dt2 in range(NTI):
                    emit_pl1(li + 1, dt2)
            if li == 0 and dbg is not None:
                P.wait("gpsimd", s_dq, s_dq.v)
                for dt_ in range(NTI):
                    P.op("gpsimd", lambda e, dt_=dt_: e.dma_start(
                        out=dbg[7, dt_ * 128:(dt_ + 1) * 128, :], in_=y2[:, dt_, :]).then_inc(s_wl.h, 16))
                    s_wl.add(16)
                for eng in ENGS:
                    P.wait(eng, s_wl, s_wl.v)
            P.wait("tensor", s_ac, v_w12[li])

            for ck in range(T // 512):
                sl = slice(ck * 512, (ck + 1) * 512)
                v_ftev = {}
                for ft in range(NTF):
                    bank = pb0[:, 0:512] if ft % 2 == 0 else pb1[:, 0:512]
                    if ft >= 2:
                        P.wait("tensor", s_ac, v_ftev[ft - 2])

                    def w1mm(e, ft, c, bank, last, li=li, sl=sl):
                        ins = e.matmul(bank, w1T(li % 2)[:, c, ft * 128:(ft + 1) * 128], y2[:, c, sl],
                                       start=(c == 0), stop=(c == NTI - 1))
                        if last:
                            ins.then_inc(s_mm.h, 1)
                    dbgfirst = (li == 0 and ck == 0 and ft == 0 and pers.get("dbg4") is not None)
                    for c in range(NTI):
                        if False and dbgfirst and c == 1:
                            P.op("tensor", lambda e: e.matmul(pg[:, 0:512], w1T(0)[:, 0, 0:128], y2[:, 0, 0:512], start=True, stop=True).then_inc(s_mm.h, 1))
                            mvx = s_mm.add(1)
                            P.wait("vector", s_mm, mvx)
                            P.op("vector", lambda e: e.tensor_copy(u2[:], pg[:, 0:512]).then_inc(s_dq.h, 1))
                            s_dq.add(1)
                            P.wait("gpsimd", s_dq, s_dq.v)
                            P.op("gpsimd", lambda e: e.dma_start(out=pers["dbg4"][3], in_=u2[:]).then_inc(s_wl.h, 16))
                            P.op("gpsimd", lambda e: e.dma_start(out=pers["dbg4"][1], in_=y2[:, 0, 0:512]).then_inc(s_wl.h, 16))
                            P.op("gpsimd", lambda e: e.dma_start(out=pers["dbg4"][2], in_=w1T(0)[:, 0, 0:512]).then_inc(s_wl.h, 16))
                            s_wl.add(48)
                            for eng in ENGS:
                                P.wait(eng, s_wl, s_wl.v)
                        P.op("tensor", lambda e, ft=ft, c=c, bank=bank, last=(c == NTI - 1),
                             li=li, sl=sl, f=w1mm: f(e, ft, c, bank, last, li, sl))
                    mv = s_mm.add(1)
                    if li == 0 and ck == 0 and ft == 0 and pers.get("dbg4") is not None:
                        P.wait("vector", s_mm, mv)
                        P.op("vector", lambda e: e.tensor_copy(u1[:], pb0[:, 0:512]).then_inc(s_dq.h, 1))
                        s_dq.add(1)
                        P.wait("gpsimd", s_dq, s_dq.v)
                        P.op("gpsimd", lambda e: e.dma_start(out=pers["dbg4"][0], in_=u1[:]).then_inc(s_wl.h, 16))
                        s_wl.add(16)
                        for eng in ENGS:
                            P.wait(eng, s_wl, s_wl.v)
                    P.wait("scalar", s_mm, mv)
                    P.op("scalar", lambda e, ft=ft, bank=bank, li=li: e.activation(
                        y3v[:, ft, :], bank, AF.Silu, bias=b1v[:, ft * NL + li:ft * NL + li + 1]).then_inc(s_ac.h, 1))
                    v_ftev[ft] = s_ac.add(1)
                    if False:
                        P.wait("gpsimd", s_ac, v_ftev[0])
                        P.op("gpsimd", lambda e: e.dma_start(out=pers["dbg4"][1], in_=y2[:, 0, 0:512]).then_inc(s_wl.h, 16))
                        pass
                        for eng in ENGS:
                            P.wait(eng, s_wl, s_wl.v)
                P.wait("tensor", s_ac, v_ftev[NTF - 1])

                def w2mm(e, dt_, c, last, li=li):
                    ps = [pg[:, 0:512], pg[:, 512:1024], pu[:, 0:512], pu[:, 512:1024]][dt_]
                    ins = e.matmul(ps, w2T(li % 2)[:, c, dt_ * 128:(dt_ + 1) * 128], y3v[:, c, :],
                                   start=(c == 0), stop=(c == NTF - 1))
                    if last:
                        ins.then_inc(s_mm.h, 1)
                for dt_ in range(NTI):
                    for c in range(NTF):
                        P.op("tensor", lambda e, dt_=dt_, c=c, last=(dt_ == NTI - 1 and c == NTF - 1),
                             li=li, f=w2mm: f(e, dt_, c, last, li))
                mv = s_mm.add(1)
                P.wait("vector", s_mm, mv)
                for dt_ in range(NTI):
                    vch(lambda e, dt_=dt_, sl=sl, li=li: e.scalar_tensor_tensor(
                        out=h[:, dt_, sl],
                        in0=[pg[:, 0:512], pg[:, 512:1024], pu[:, 0:512], pu[:, 512:1024]][dt_],
                        scalar=vq["b2"][:, dt_ * NL + li:dt_ * NL + li + 1], in1=h[:, dt_, sl],
                        op0=ALU.add, op1=ALU.add))
                vv = s_dq.v
                P.wait("tensor", s_dq, vv)
                if li == 0 and ck == 0 and pers.get("dbg2") is not None:
                    P.wait("gpsimd", s_dq, vv)
                    P.op("gpsimd", lambda e: e.dma_start(
                        out=pers["dbg2"].rearrange("(c k) t -> k c t", k=128),
                        in_=y3v[:]).then_inc(s_wl.h, 16))
                    s_wl.add(16)
                    for eng in ENGS:
                        P.wait(eng, s_wl, s_wl.v)
            dump_h(1 + li, s_dq.v)

        woutsrc = W["WoutT"]
        for c in range(4):
            P.wait("gpsimd", s_ac, relst[1])
            P.op("gpsimd", lambda e, c=c: e.dma_start(
                out=wrawA[:], in_=woutsrc[c * 128:(c + 1) * 128, :]).then_inc(s_wb[2].h, 16))
            wv = s_wb[2].add(16)
            P.wait("scalar", s_wb[2], wv)
            P.op("scalar", lambda e, c=c: e.activation(
                woutT[:, c, :], wrawA[:], AF.Copy).then_inc(s_ac.h, 1))
            relst[1] = s_ac.add(1)
        P.wait("tensor", s_ac, relst[1])
        P.wait("vector", s_ac, relst[1])

        def hl_out(dt_, ck, vv):
            P.op("vector", lambda e, dt_=dt_, ck=ck: e.tensor_scalar(
                out=hl[:, dt_, ck * 512:(ck + 1) * 512], in0=u2[:],
                scalar1=lnlb[:, dt_:dt_ + 1], scalar2=None, op0=ALU.add).then_inc(s_dq.h, 1))
            v = s_dq.add(1)
            P.wait("vector", s_dq, v)
        emit_ln(lambda dt_: lnlw[:, dt_:dt_ + 1], T, 0, hl_out)
        v_hl = s_dq.v
        P.wait("tensor", s_dq, v_hl)
        if pers.get("dbg") is not None:
            P.wait("gpsimd", s_dq, v_hl)
            for dt_ in range(NTI):
                P.op("gpsimd", lambda e, dt_=dt_: e.dma_start(
                    out=pers["dbg"][5, dt_ * 128:(dt_ + 1) * 128, 0:T], in_=hl[:, dt_, :]).then_inc(s_wl.h, 16))
                s_wl.add(16)
            for eng in ENGS:
                P.wait(eng, s_wl, s_wl.v)

        if pers.get("dbg2") is not None:
            P.wait("gpsimd", s_dq, s_dq.v)
            P.op("gpsimd", lambda e: e.dma_start(
                out=pers["dbg2"].rearrange("(a k) d -> k a d", k=128)[:, 0:4, :],
                in_=woutT[:, 0, 0:2048].rearrange("k (a d) -> k a d", d=512)).then_inc(s_wl.h, 16))
            s_wl.add(16)
            for eng in ENGS:
                P.wait(eng, s_wl, s_wl.v)
        v_aoev = {}
        for vt in range(NTV):
            ps = pg if vt % 2 == 0 else pu
            if vt >= 2:
                P.wait("tensor", s_dq, v_aoev[vt - 2])

            def womm(e, vt, c, nb, ps, last):
                ins = e.matmul(ps[:, nb * 512:(nb + 1) * 512],
                               woutT[:, c, vt * 128:(vt + 1) * 128],
                               hl[:, c, nb * 512:(nb + 1) * 512],
                               start=(c == 0), stop=(c == NTI - 1))
                if last:
                    ins.then_inc(s_mm.h, 1)
            for c in range(NTI):
                for nb in range(2):
                    P.op("tensor", lambda e, vt=vt, c=c, nb=nb, ps=ps,
                         last=(c == NTI - 1 and nb == 1): womm(e, vt, c, nb, ps, last))
            mv = s_mm.add(1)
            P.wait("vector", s_mm, mv)
            P.op("vector", lambda e, vt=vt, ps=ps: e.tensor_scalar(
                out=wostg[:], in0=ps[:], scalar1=boutv[:, vt:vt + 1], scalar2=None,
                op0=ALU.add).then_inc(s_dq.h, 1))
            av = s_dq.add(1)
            P.wait("tensor", s_dq, av)

            def aotr(e, tsub, last):
                ins = e.transpose(pA[:].bitcast(BF16)[:, tsub * 128:(tsub + 1) * 128],
                                  wostg[:].rearrange("k (c j) -> k c j", j=128)[:, tsub, :], ident_b[:])
                if last:
                    ins.then_inc(s_mm.h, 1)
            for tsub in range(8):
                P.op("tensor", lambda e, tsub=tsub, last=(tsub == 7): aotr(e, tsub, last))
            mv = s_mm.add(1)
            P.wait("vector", s_mm, mv)

            def aoev(e, tsub, vt, last):
                ins = e.tensor_copy(ao_t[:, tsub, vt * 128:(vt + 1) * 128],
                                    pA[:].bitcast(BF16)[:, tsub * 128:(tsub + 1) * 128])
                if last:
                    ins.then_inc(s_dq.h, 1)
            for tsub in range(8):
                P.op("vector", lambda e, tsub=tsub, vt=vt, last=(tsub == 7): aoev(e, tsub, vt, last))
            v_aoev[vt] = s_dq.add(1)
            P.wait("tensor", s_dq, v_aoev[vt])
        v_ao = v_aoev[NTV - 1]
        if pers.get("dbg") is not None:
            P.wait("gpsimd", s_dq, v_ao)
            P.op("gpsimd", lambda e: e.dma_start(
                out=pers["dbg"][6, 0:128, 0:2048], in_=ao_t[:, 0, 0:2048]).then_inc(s_wl.h, 16))
            P.op("gpsimd", lambda e: e.dma_start(
                out=pers["dbg"][6, 128:256, 0:2048], in_=ao_t[:, 0, 2048:4096]).then_inc(s_wl.h, 16))
            s_wl.add(32)
            for eng in ENGS:
                P.wait(eng, s_wl, s_wl.v)

        P.wait("sync", s_dq, v_ao)
        for tt in range(8):
            P.op("sync", lambda e, tt=tt: e.dma_start(
                out=x0st[:], in_=x_own[tt * 128:(tt + 1) * 128, :]).then_inc(s_ld.h, 16))
            lv = s_ld.add(16)
            P.wait("vector", s_ld, lv)
            vch(lambda e, tt=tt: e.tensor_tensor(out=r1f[:], in0=ao_t[:, tt, :], in1=x0st[:], op=ALU.add))
            vch(lambda e: e.reduce_sum(out=colA[:], in_=r1f[:], axis=AXX))
            vv = s_dq.v
            P.wait("scalar", s_dq, vv)
            P.op("scalar", lambda e: e.activation(sqf[:], r1f[:], AF.Square).then_inc(s_ac.h, 1))
            av = s_ac.add(1)
            P.wait("vector", s_ac, av)
            vch(lambda e: e.reduce_sum(out=colB[:], in_=sqf[:], axis=AXX))
            vch(lambda e: e.tensor_scalar(out=colA[:], in0=colA[:], scalar1=1.0 / DM, scalar2=None, op0=ALU.mult))
            vch(lambda e: e.scalar_tensor_tensor(out=colC[:], in0=colA[:], scalar=0.0, in1=colA[:], op0=ALU.add, op1=ALU.mult))
            vch(lambda e: e.scalar_tensor_tensor(out=colB[:], in0=colB[:], scalar=1.0 / DM, in1=colC[:], op0=ALU.mult, op1=ALU.subtract))
            vv = vch(lambda e: e.tensor_scalar(out=colB[:], in0=colB[:], scalar1=EPS, scalar2=None, op0=ALU.add))
            P.wait("scalar", s_dq, vv)
            P.op("scalar", lambda e: e.activation(colB[:], colB[:], AF.Sqrt).then_inc(s_ac.h, 1))
            av = s_ac.add(1)
            P.wait("vector", s_ac, av)
            vch(lambda e: e.reciprocal(colB[:], colB[:]))
            vch(lambda e: e.tensor_tensor(out=colC[:], in0=colA[:], in1=colB[:], op=ALU.mult))
            vx = vch(lambda e: e.scalar_tensor_tensor(
                out=r1f[:], in0=r1f[:], scalar=colB[:], in1=colC[:].broadcast_to([128, DM]),
                op0=ALU.mult, op1=ALU.subtract))
            vfree = x1_tile_hook(tt, r1f, vx)
            P.wait("vector", s_dq, vfree) if isinstance(vfree, int) and vfree else None
            P.wait("sync", s_dq, vx)
    return



def build(nc):
    x_own = nc.dram_tensor("x_own", [T, DM], F32, kind="ExternalInput").ap()
    hsel = nc.dram_tensor("hsel", [128, 2], F32, kind="ExternalInput").ap()
    anames = {"bin": [DI], "ln1_w": [NL, DI], "ln1_b": [NL, DI],
              "phz_re": [NL, DI], "phz_im": [NL, DI], "lc_re": [NL, DI], "lc_im": [NL, DI],
              "sc_lin": [NL, DI], "ln2_w": [NL, DI], "ln2_b": [NL, DI],
              "b1": [NL, DFF], "b2": [NL, DI],
              "lnl_w": [DI], "lnl_b": [DI], "bout": [DM]}
    W = {}
    for n, shp in anames.items():
        ap = nc.dram_tensor("L0_attn_" + n, shp, F32, kind="ExternalInput").ap()
        W[n] = ap
    W["WinT"] = nc.dram_tensor("xWinT", [DM, DI], F32, kind="ExternalInput").ap()
    w1t_ap = nc.dram_tensor("xW1T", [NL, DI, DFF], F32, kind="ExternalInput").ap()
    w2t_ap = nc.dram_tensor("xW2T", [NL, DFF, DI], F32, kind="ExternalInput").ap()
    W["W1T"] = [w1t_ap[i] for i in range(NL)]
    W["W2T"] = [w2t_ap[i] for i in range(NL)]
    W["WoutT"] = nc.dram_tensor("xWoutT", [DI, DM], F32, kind="ExternalInput").ap()
    x_ownT = nc.dram_tensor("x_ownT", [DM, T], F32, kind="ExternalInput").ap()
    qw_g = nc.dram_tensor("L0_ffn_qw_gate", [D, H], I32, kind="ExternalInput").ap()
    z_g = nc.dram_tensor("L0_ffn_z_gate", [32, H], I32, kind="ExternalInput").ap()
    s_g = nc.dram_tensor("L0_ffn_s_gate", [32, H], F32, kind="ExternalInput").ap()
    qw_u = nc.dram_tensor("L0_ffn_qw_up", [D, H], I32, kind="ExternalInput").ap()
    z_u = nc.dram_tensor("L0_ffn_z_up", [32, H], I32, kind="ExternalInput").ap()
    s_u = nc.dram_tensor("L0_ffn_s_up", [32, H], F32, kind="ExternalInput").ap()
    qw_d = nc.dram_tensor("L0_ffn_qw_down", [H, D], I32, kind="ExternalInput").ap()
    z_d = nc.dram_tensor("L0_ffn_z_down", [HT, D], I32, kind="ExternalInput").ap()
    s_d = nc.dram_tensor("L0_ffn_s_down", [HT, D], F32, kind="ExternalInput").ap()
    out = nc.dram_tensor("out", [T, D], F32, kind="ExternalOutput").ap()
    x1_sp = nc.dram_tensor("x1_sp", [T, D], F32, kind="Internal").ap()
    cc_in = nc.dram_tensor("cc_in", [128, 8], F32, kind="Internal").ap()
    cc_out = nc.dram_tensor("cc_out", [256, 8], F32, kind="Internal").ap()
    act_sp = nc.dram_tensor("act_sp", [H, T], BF16, kind="Internal").ap()
    r2_sp = nc.dram_tensor("r2_sp", [T, D], F32, kind="Internal").ap()

    es = ExitStack()
    with es:
        mk = lambda name, shape, dt: es.enter_context(nc.sbuf_tensor(name, shape, dt))
        mkp = lambda name, shape, dt: es.enter_context(nc.psum_tensor(name, shape, dt))
        P = Plan()
        sems = {k: Sem(nc, es, "s_" + k) for k in (
            "ld", "sc", "wl", "mm", "dq", "bc", "ac", "st", "gp",
            "wb0", "wb1", "wb2", "sp0", "sp1", "sp2", "ab0", "ab1",
            "wd", "r2", "fo0", "fo1", "fl0", "fl1", "ci", "co", "cc", "ar")}
        pers = {}
        pers["pA"] = mkp("pA", [128, 1024], F32)
        pers["pb0"] = mkp("pb0", [128, 512], F32)
        pers["pb1"] = mkp("pb1", [128, 512], F32)
        pers["pg"] = mkp("pg", [128, 1024], F32)
        pers["pu"] = mkp("pu", [128, 1024], F32)
        pers["ident_b"] = mk("ident_b", [128, 128], BF16)
        pers["ident_f"] = mk("ident_f", [128, 128], F32)
        pers["iot"] = mk("iot", [128, 128], I32)
        P.op("gpsimd", lambda e: e.iota(pers["iot"][:], pattern=[[-1, 128]], base=0, channel_multiplier=1).then_inc(sems["gp"].h, 1))
        pers["v_iota"] = sems["gp"].add(1)
        P.wait("vector", sems["gp"], pers["v_iota"])
        P.op("vector", lambda e: e.tensor_scalar(out=pers["ident_f"][:], in0=pers["iot"][:], scalar1=0.0, scalar2=None, op0=ALU.is_equal))
        P.op("vector", lambda e: e.tensor_scalar(out=pers["ident_b"][:], in0=pers["iot"][:], scalar1=0.0, scalar2=None, op0=ALU.is_equal).then_inc(sems["dq"].h, 1))
        pers["v_identf"] = sems["dq"].add(1)
        pers["v_ident"] = pers["v_identf"]
        pers["cc_in"] = cc_in
        pers["cc_out"] = cc_out

        s_st = sems["st"]
        s_dq = sems["dq"]

        def hook(tt, buf, v_ready):
            P.wait("sync", s_dq, v_ready)
            P.op("sync", lambda e, tt=tt, buf=buf: e.dma_start(
                out=x1_sp[tt * 128:(tt + 1) * 128, :], in_=buf[:]).then_inc(s_st.h, 16))
            sv = s_st.add(16)
            P.wait("vector", s_st, sv)
            return 0

        emit_attn(nc, P, es, x_ownT, x_own, hsel, W, sems, pers, hook)

        for eng in ENGS:
            for sm in sems.values():
                P.wait(eng, sm, sm.v)

        emit_ffn(nc, P, es, x1_sp, qw_g, z_g, s_g, qw_u, z_u, s_u, qw_d, z_d, s_d,
                 out, act_sp, r2_sp, sems, pers)

        block = es.enter_context(nc.Block())
        P.replay(block)
    return nc


_NC_CACHE = {}


def _get_nc():
    if "nc" not in _NC_CACHE:
        nc = bass.Bass("TRN2", target_bir_lowering=False)
        build(nc)
        _NC_CACHE["nc"] = nc
    return _NC_CACHE["nc"]


def kernel(**inputs):
    nc = _get_nc()
    xfull = np.ascontiguousarray(inputs["L0_attn_x"], dtype=np.float32)
    drop = {"L0_attn_x", "L0_attn_Win", "L0_attn_W1", "L0_attn_W2", "L0_attn_Wout"}
    wkeys = [k for k in inputs if k not in drop]
    winT_np = np.ascontiguousarray(np.asarray(inputs["L0_attn_Win"]).T)
    w1T_np = np.ascontiguousarray(np.asarray(inputs["L0_attn_W1"]).transpose(0, 2, 1))
    w2T_np = np.ascontiguousarray(np.asarray(inputs["L0_attn_W2"]).transpose(0, 2, 1))
    woutT_np = np.ascontiguousarray(np.asarray(inputs["L0_attn_Wout"]).T)
    in_maps = []
    for core in range(8):
        b, hf = core // 2, core % 2
        hs = np.zeros((128, 2), np.float32)
        hs[:, hf] = 1.0
        m = {"x_own": np.ascontiguousarray(xfull[b, hf * T:(hf + 1) * T]),
             "x_ownT": np.ascontiguousarray(xfull[b, hf * T:(hf + 1) * T].T),
             "xWinT": winT_np, "xW1T": w1T_np, "xW2T": w2T_np, "xWoutT": woutT_np,
             "hsel": hs}
        for k in wkeys:
            arr = inputs[k]
            if not (isinstance(arr, np.ndarray) and arr.flags.c_contiguous):
                arr = np.ascontiguousarray(arr)
                inputs[k] = arr
            m[k] = arr
        in_maps.append(m)
    res = run_bass_kernel_spmd(nc, in_maps, core_ids=list(range(8)))
    out = np.empty((4, S, DM), dtype=np.float32)
    for core in range(8):
        b, hf = core // 2, core % 2
        out[b, hf * T:(hf + 1) * T] = res.results[core]["out"]
    return out
